# version 07
# baseline (speedup 1.0000x reference)
import numpy as np
import ml_dtypes
import concourse.bass as bass
import concourse.mybir as mybir
from concourse.bass_utils import run_bass_kernel_spmd
from contextlib import ExitStack

F32 = mybir.dt.float32
BF16 = mybir.dt.bfloat16
I32 = mybir.dt.int32
ALU = mybir.AluOpType
AF = mybir.ActivationFunctionType
EPS = 1e-5

B, T, D = 2, 16384, 1024
H, DH, C = 16, 64, 128
HID = 3072
DC = D // 128
HO = HID // 128
TT = 512
NCH = TT // C
NF = 16
NO = 8
NP = 8
AUG = DH + 1
PW = 2 * AUG
SQW = NP * PW

DEBUG = False
RACE_DETECT = True
SIM_LOCAL_SYNC = False


NLANE = 4


class Ctx:

    def __init__(self, live, eng, sems):
        self.live = live
        self.eng = eng
        self.sems = sems
        self.c = {k: 0 for k in sems}
        self.lane_i = {"y": 0, "g": 0}

    def op(self, eng, fn, inc=0, sem=None):
        if sem is None:
            sem = eng
        chain = eng in ("v", "s") and sem == eng
        if chain:
            if self.live == eng and self.c[eng] > 0:
                self.eng.wait_ge(self.sems[eng], self.c[eng])
            inc = 1
        if self.live == eng:
            inst = fn(self.eng)
            if inc:
                inst.then_inc(self.sems[sem], inc)
        if inc:
            self.c[sem] += inc
        return (sem, self.c[sem])

    def dma(self, eng, fn):
        lane = "%s%d" % (eng, self.lane_i[eng] % NLANE)
        self.lane_i[eng] += 1
        self.wait(eng, (lane, self.c[lane]))
        return self.op(eng, fn, 16, sem=lane)

    def wait(self, eng, tok):
        if tok is None:
            return
        sem, val = tok
        if val <= 0:
            return
        if self.live == eng:
            self.eng.wait_ge(self.sems[sem], val)

    def tok(self, sem):
        return (sem, self.c[sem])


def build_program():
    nc = bass.Bass("TRN2", target_bir_lowering=False,
                   detect_race_conditions=RACE_DETECT)

    xt = nc.dram_tensor("xt", [D, NF * TT], F32, kind="ExternalInput").ap()
    wgT_i = nc.dram_tensor("wgT_i", [D, HID], I32, kind="ExternalInput").ap()
    wuT_i = nc.dram_tensor("wuT_i", [D, HID], I32, kind="ExternalInput").ap()
    wdT_i = nc.dram_tensor("wdT_i", [HID, D], I32, kind="ExternalInput").ap()
    qkvT_i = nc.dram_tensor("qkvT", [D, 3 * D], F32, kind="ExternalInput").ap()
    gateT_i = nc.dram_tensor("gateT", [D, D], F32, kind="ExternalInput").ap()
    projT_i = nc.dram_tensor("projT", [D, D], F32, kind="ExternalInput").ap()
    qkv_bc_i = nc.dram_tensor("qkv_bc", [128, 16], F32, kind="ExternalInput").ap()
    vb_row_i = nc.dram_tensor("vb_row", [1, D], F32, kind="ExternalInput").ap()
    gate_bc_i = nc.dram_tensor("gate_bc", [128, DC], F32, kind="ExternalInput").ap()
    proj_bc_i = nc.dram_tensor("proj_bc", [128, DC], F32, kind="ExternalInput").ap()
    ln_gc_i = nc.dram_tensor("ln_gc", [128, DC], F32, kind="ExternalInput").ap()
    ln_bc_i = nc.dram_tensor("ln_bc", [128, DC], F32, kind="ExternalInput").ap()
    mcols_i = nc.dram_tensor("mcols", [128, 3], F32, kind="ExternalInput").ap()
    mask_i = nc.dram_tensor("mask_tri", [C, C], F32, kind="ExternalInput").ap()
    consts_i = nc.dram_tensor("consts_i", [128, 2], F32, kind="ExternalInput").ap()
    nonce_i = nc.dram_tensor("nonce_i", [1, 16], I32, kind="ExternalInput").ap()
    ident_i = nc.dram_tensor("ident", [C, C], BF16, kind="ExternalInput").ap()
    out_t = nc.dram_tensor("out_t", [D, NO * TT], F32, kind="ExternalOutput").ap()

    wg_sw = nc.dram_tensor("wg_sw", [128, HO * 1024], BF16)
    wu_sw = nc.dram_tensor("wu_sw", [128, HO * 1024], BF16)
    wd_sw = nc.dram_tensor("wd_sw", [128, DC * HID], BF16)
    qk_sw = nc.dram_tensor("qk_sw", [128, 16 * 1024], BF16)
    v_sw = nc.dram_tensor("v_sw", [128, DC * D], BF16)
    gate_sw = nc.dram_tensor("gate_sw", [128, DC * 1024], BF16)
    proj_sw = nc.dram_tensor("proj_sw", [128, DC * 1024], BF16)
    bx2 = nc.dram_tensor("bx2", [D, NO * TT], F32)
    bq = nc.dram_tensor("bq", [D, NO * TT], BF16)
    bk = nc.dram_tensor("bk", [D, NO * TT], BF16)
    bva = nc.dram_tensor("bva", [128, NO * NCH * SQW], BF16)
    shd = nc.dram_tensor("shd", [512, SQW], F32, addr_space="Shared")
    shf = nc.dram_tensor("shf", [2, 16], I32, addr_space="Shared")

    ctxm = ExitStack()
    sbuf = lambda name, shape, dt: ctxm.enter_context(nc.sbuf_tensor(name, shape, dt))
    psum = lambda name, shape, dt: ctxm.enter_context(nc.psum_tensor(name, shape, dt))

    x_tA = sbuf("x_tA", [128, DC * TT], F32)
    xn = sbuf("xn", [128, DC * TT], BF16)
    tmp = sbuf("tmp", [128, DC * TT], BF16)
    y2b = sbuf("y2b", [128, DC * TT], BF16)
    gn = sbuf("gn", [128, DC * TT], BF16)
    Q = sbuf("Q", [128, DC * TT], BF16)
    K = sbuf("K", [128, DC * TT], BF16)
    v_aug = sbuf("v_aug", [128, NCH * SQW], BF16)
    h_sb = sbuf("h_sb", [128, HO * TT], BF16)
    k_tm = sbuf("k_tm", [128, NCH * NP * C], BF16)
    s_Am = sbuf("s_Am", [128, 2 * C], BF16)
    S_sb0 = sbuf("S_sb0", [128, H * AUG], BF16)
    S_sb1 = sbuf("S_sb1", [128, H * AUG], BF16)
    S_sbs = [S_sb0, S_sb1]
    S_start = sbuf("S_start", [128, SQW], F32)
    s_rec = sbuf("s_rec", [128, SQW], F32)
    s_own = sbuf("s_own", [128, SQW], F32)
    E2p = sbuf("E2p", [128, SQW], F32)
    E1p = s_own
    rows_f = sbuf("rows_f", [1, 5 * TT], F32)
    rows_b = sbuf("rows_b", [1, 4 * TT], BF16)
    wbuf_g = sbuf("wbuf_g", [128, 3 * 1024], BF16)
    wbuf_u = sbuf("wbuf_u", [128, 3 * 1024], BF16)
    wd_sb = sbuf("wd_sb", [128, 2 * HID], BF16)
    wv_sb = sbuf("wv_sb", [128, DC * D], BF16)
    wo_sb = sbuf("wo_sb", [128, 2 * 1024], BF16)
    wint = sbuf("wint", [128, HID], I32)
    wcast = sbuf("wcast", [128, HID], BF16)
    _wint_off = nc.lookup_mloc(wint).addr
    x_tB = nc.alloc_sbuf_tensor_at("x_tB", [128, DC * TT], F32, offset=_wint_off)
    x_ts = [x_tA, x_tB]
    qkv_bc = sbuf("qkv_bcs", [128, 16], F32)
    gate_bc = sbuf("gate_bcs", [128, DC], F32)
    proj_bc = sbuf("proj_bcs", [128, DC], F32)
    ln_gc = sbuf("ln_gcs", [128, DC], F32)
    ln_bc = sbuf("ln_bcs", [128, DC], F32)
    mcols = sbuf("mcolss", [128, 3], F32)
    vbias = sbuf("vbias", [128, D], F32)
    mask_tri = sbuf("mask_tris", [C, C], F32)
    ident = sbuf("idents", [C, C], BF16)
    onescol = sbuf("onescol", [128, 1], BF16)
    negcol = sbuf("negcol", [128, 1], BF16)
    ones1r = sbuf("ones1r", [1, 128], BF16)
    consts_sb = sbuf("consts_sb", [128, 2], F32)
    nc.const_aps.register(F32, EPS, consts_sb[:, 0:1])
    nc.const_aps.register(F32, -0.5, consts_sb[:, 1:2])
    den_sb = sbuf("den_sb", [128, 16], F32)
    rd_sb = sbuf("rd_sb", [128, 16], F32)

    pA = psum("pA", [128, 512], F32)
    pB = psum("pB", [128, 512], F32)
    pC = psum("pC", [128, 512], F32)
    pD = psum("pD", [128, 512], F32)
    pT = psum("pT", [128, 1024], BF16)
    pS0 = psum("pS0", [128, 3 * PW], F32)
    pS1 = psum("pS1", [128, 3 * PW], F32)
    pS2 = psum("pS2", [128, 512], F32)
    pS = [pS0, pS1, pS2]

    def pS_blk(p):
        return pS[p // 3][:, (p % 3) * PW:(p % 3) * PW + PW]

    dbg = {}
    if DEBUG:
        dbg["d_x2"] = nc.dram_tensor("d_x2", [128, DC * TT], F32, kind="ExternalOutput").ap()
        for nme in ["d_xn2", "d_q", "d_k", "d_gn"]:
            dbg[nme] = nc.dram_tensor(nme, [128, DC * TT], BF16, kind="ExternalOutput").ap()
        dbg["d_va"] = nc.dram_tensor("d_va", [128, NCH * SQW], BF16, kind="ExternalOutput").ap()
        dbg["d_sstart"] = nc.dram_tensor("d_sstart", [128, SQW], F32, kind="ExternalOutput").ap()
        dbg["d_otm"] = nc.dram_tensor("d_otm", [128, 1024], BF16, kind="ExternalOutput").ap()

    def schedule(cx: Ctx):
        op, wait, tok = cx.op, cx.wait, cx.tok
        dma = cx.dma
        st8 = {"upd_first": True, "SrecEvac": None, "wv_free": None,
               "bq": None, "bk": None, "bva": None,
               "wgu_free": [None, None], "wd_free": [None, None],
               "wo_free": [None, None]}

        if cx.live == "g":
            e = cx.eng
            cid = e.partition_id()
            par = cid % 2
            opar = (cid + 1) % 2
            my_flag = shf.ap()[bass.ts(par, 1), 0:1]
            pa_flag = shf.ap()[bass.ts(opar, 1), 0:1]
            st8["g"] = (par, opar, my_flag, pa_flag)

        _ct = [
            dma("y", lambda e: e.dma_start(out=mask_tri[:, :], in_=mask_i)),
            dma("y", lambda e: e.dma_start(out=ident[:, :], in_=ident_i)),
            dma("y", lambda e: e.dma_start(out=qkv_bc[:, :], in_=qkv_bc_i)),
            dma("y", lambda e: e.dma_start(out=gate_bc[:, :], in_=gate_bc_i)),
            dma("y", lambda e: e.dma_start(out=proj_bc[:, :], in_=proj_bc_i)),
            dma("y", lambda e: e.dma_start(out=ln_gc[:, :], in_=ln_gc_i)),
            dma("y", lambda e: e.dma_start(out=ln_bc[:, :], in_=ln_bc_i)),
            dma("y", lambda e: e.dma_start(out=mcols[:, :], in_=mcols_i)),
            dma("y", lambda e: e.dma_start(out=consts_sb[:, :], in_=consts_i)),
        ]
        tok_consts = _ct[-NLANE:]
        op("v", lambda e: e.memset(onescol[:, :], 1.0))
        op("v", lambda e: e.memset(negcol[:, :], -1.0))
        op("v", lambda e: e.memset(ones1r[:, :], 1.0))
        op("v", lambda e: e.memset(v_aug[:, :], 1.0))
        op("v", lambda e: e.memset(S_sb0[:, :], 0.0), 1)
        tok_init_v = op("v", lambda e: e.memset(S_sb1[:, :], 0.0), 1)

        tv = None
        tg = None
        for wi, wsw, scl, cols, pat in (
                (wgT_i, wg_sw, 1.0 / 127.0, HID, "g"),
                (wuT_i, wu_sw, 1.0 / 127.0, HID, "g"),
                (wdT_i, wd_sw, 1.0 / 254.0, D, "d")):
            nchunks = DC if cols == HID else HO
            for c in range(nchunks):
                if tv is not None:
                    wait("y", tv)
                tk = dma("y", lambda e, wi=wi, c=c, cols=cols: e.dma_start(
                    out=wint[:, 0:cols], in_=wi[128 * c:128 * (c + 1), :]))
                wait("v", tk)
                if tg is not None:
                    wait("v", tg)
                tv = op("v", lambda e, scl=scl, cols=cols: e.tensor_scalar_mul(
                    wcast[:, 0:cols], wint[:, 0:cols], scl), 1)
                wait("g", tv)
                if pat == "g":
                    tg = dma("g", lambda e, wsw=wsw, c=c: e.dma_start(
                        out=wsw.ap().rearrange("p (ho c i) -> p ho c i",
                                               ho=HO, c=DC)[:, :, c, :],
                        in_=wcast.ap().rearrange("p (ho i) -> p ho i", ho=HO)))
                else:
                    tg = dma("g", lambda e, c=c: e.dma_start(
                        out=wd_sw.ap().rearrange("p (do hc i) -> p do hc i",
                                                 do=DC, hc=HO)[:, :, c, :],
                        in_=wcast[:, 0:D].rearrange("p (do i) -> p do i", do=DC)))
        lasty = []
        for c in range(DC):
            wait("g", tv)
            wait("g", tg)
            for _t in lasty:
                wait("g", _t)
            tg = dma("g", lambda e, c=c: e.dma_start(
                out=wcast[:, :], in_=qkvT_i[128 * c:128 * (c + 1), :]))
            wait("y", tg)
            _q1 = dma("y", lambda e, c=c: e.dma_start(
                out=qk_sw.ap().rearrange("p (o c i) -> p o c i", o=16, c=DC)[:, :, c, :],
                in_=wcast[:, 0:2048].rearrange("p (o i) -> p o i", o=16)))
            _q2 = dma("y", lambda e, c=c: e.dma_start(
                out=v_sw.ap()[:, D * c:D * (c + 1)], in_=wcast[:, 2048:3072]))
            lasty = [_q1, _q2]
        for src, dst in ((gateT_i, gate_sw), (projT_i, proj_sw)):
            for c in range(DC):
                for _t in lasty:
                    wait("g", _t)
                tg = dma("g", lambda e, src=src, c=c: e.dma_start(
                    out=wcast[:, 0:D], in_=src[128 * c:128 * (c + 1), :]))
                wait("y", tg)
                lasty = [dma("y", lambda e, dst=dst, c=c: e.dma_start(
                    out=dst.ap().rearrange("p (o c i) -> p o c i", o=DC, c=DC)[:, :, c, :],
                    in_=wcast[:, 0:D].rearrange("p (o i) -> p o i", o=DC)))]
        for _t in lasty:
            wait("g", _t)
        tg = dma("g", lambda e: e.dma_start(out=wcast[0:1, 0:D], in_=vb_row_i))
        wait("t", tg)
        wait("t", tok_init_v)
        tvb = None
        for half in range(2):
            if tvb is not None:
                wait("t", tvb)
            tth = op("t", lambda e, half=half: e.matmul(
                pA[:, :], ones1r[:, :], wcast[0:1, 512 * half:512 * (half + 1)],
                start=True, stop=True, skip_group_check=True), 1)
            wait("v", tth)
            tvb = op("v", lambda e, half=half: e.tensor_copy(
                vbias[:, 512 * half:512 * (half + 1)], pA[:, :]), 1)
        tok_vbias = tvb
        dequant_done = [tok_vbias, tok("t")] + list(lasty)

        st8["wv_free"] = tok("t")
        st8["wgu_free"] = [tok("t"), tok("t"), tok("t")]
        st8["wd_free"] = [tok("t"), tok("t")]
        st8["wo_free"] = [tok("t"), tok("t")]

        def ln_pass(xb, sq_scratch, out_buf, extra=None, pfree=None,
                    pre_s=(), out_ready=None):
            R = lambda i: rows_f[0:1, TT * i:TT * (i + 1)]
            RB = lambda i: rows_b[0:1, TT * i:TT * (i + 1)]
            for tk in pre_s:
                wait("s", tk)
            ts1 = op("s", lambda e: e.activation(tmp[:, :], xb[:, :], AF.Copy), 1)
            ts2 = op("s", lambda e, b=sq_scratch: e.activation(
                b[:, :], tmp[:, :], AF.Square), 1)
            wait("t", ts1)
            wait("t", pfree)
            for c in range(DC):
                op("t", lambda e, c=c: e.matmul(
                    pC[0:1, :], negcol[:, :], tmp[:, TT * c:TT * (c + 1)],
                    start=(c == 0), stop=(c == DC - 1), skip_group_check=True))
            wait("t", ts2)
            for c in range(DC):
                tt = op("t", lambda e, c=c, b=sq_scratch: e.matmul(
                    pD[0:1, :], onescol[:, :], b[:, TT * c:TT * (c + 1)],
                    start=(c == 0), stop=(c == DC - 1), skip_group_check=True),
                        1 if c == DC - 1 else 0)
            wait("v", tt)
            tvr = op("v", lambda e: e.tensor_scalar_mul(RB(0), pC[0:1, :], 1.0 / D), 1)
            op("v", lambda e: e.tensor_mul(R(2), RB(0), RB(0)))
            tvv2 = op("v", lambda e: e.scalar_tensor_tensor(
                R(3), pD[0:1, :], 1.0 / D, R(2), ALU.mult, ALU.subtract), 1)
            wait("s", tvv2)
            op("s", lambda e: e.activation(R(4), R(3), AF.Ln, bias=EPS))
            ts3 = op("s", lambda e: e.activation(RB(1), R(4), AF.Exp, scale=-0.5), 1)
            if extra == "ln2":
                op("v", lambda e: e.tensor_scalar_add(R(1), R(3), EPS))
                tv2 = op("v", lambda e: e.reciprocal(R(2), R(1)), 1)
                tv2b = op("v", lambda e: e.tensor_mul(R(1), R(3), R(2)), 1)
                wait("s", tv2b)
                op("s", lambda e: e.activation(R(2), R(1), AF.Ln, bias=EPS))
                ts4 = op("s", lambda e: e.activation(R(1), R(2), AF.Exp, scale=-0.5), 1)
                wait("v", ts4)
                tva = op("v", lambda e: e.tensor_mul(RB(2), RB(1), R(1)), 1)
            wait("t", tvv2)
            if extra == "ln2":
                wait("t", tv2b)
            tt2 = op("t", lambda e: e.matmul(pD[:, :], ones1r[:, :], RB(0),
                                             start=True, stop=True,
                                             skip_group_check=True))
            wait("t", ts3)
            tt3 = op("t", lambda e: e.matmul(pC[:, :], ones1r[:, :], RB(1),
                                             start=True, stop=True,
                                             skip_group_check=True),
                     0 if extra == "ln2" else 1)
            if extra == "ln2":
                wait("t", tva)
                tt3 = op("t", lambda e: e.matmul(pB[:, :], ones1r[:, :], RB(2),
                                                 start=True, stop=True,
                                                 skip_group_check=True), 1)
            wait("v", tt3)
            if out_ready is not None:
                wait("v", out_ready)
            chunk_toks = []
            y2_toks = []
            if extra == "ln2":
                for c in range(DC):
                    op("v", lambda e, c=c: e.tensor_add(
                        tmp[:, TT * c:TT * (c + 1)], xb[:, TT * c:TT * (c + 1)],
                        pD[:, :]))
                    y2_toks.append(op("v", lambda e, c=c: e.tensor_mul(
                        y2b[:, TT * c:TT * (c + 1)], tmp[:, TT * c:TT * (c + 1)],
                        pC[:, :]), 1))
                ug = None
                for c in range(DC):
                    op("v", lambda e, c=c: e.tensor_mul(
                        gn[:, TT * c:TT * (c + 1)], tmp[:, TT * c:TT * (c + 1)],
                        pB[:, :]))
                    if c == DC - 1:
                        ug = op("v", lambda e: e.engine_nop(), 1)
                    chunk_toks.append(op("v", lambda e, c=c, ob=out_buf: e.tensor_scalar(
                        ob[:, TT * c:TT * (c + 1)], gn[:, TT * c:TT * (c + 1)],
                        ln_gc[:, c:c + 1], ln_bc[:, c:c + 1], ALU.mult, ALU.add), 1))
                return chunk_toks[-1], chunk_toks, y2_toks, ug
            else:
                for c in range(DC):
                    op("v", lambda e, c=c: e.tensor_add(
                        tmp[:, TT * c:TT * (c + 1)], xb[:, TT * c:TT * (c + 1)],
                        pD[:, :]))
                    chunk_toks.append(op("v", lambda e, c=c, ob=out_buf: e.tensor_mul(
                        ob[:, TT * c:TT * (c + 1)], tmp[:, TT * c:TT * (c + 1)],
                        pC[:, :]), 1))
                return chunk_toks[-1], chunk_toks, None, None

        def ffn_pass(xb, xn_toks):
            tok_h = []
            for ho in range(HO):
                par = ho % 3
                wait("y", st8["wgu_free"][par])
                tyg = dma("y", lambda e, ho=ho, par=par: e.dma_start(
                    out=wbuf_g[:, 1024 * par:1024 * (par + 1)],
                    in_=wg_sw.ap()[:, 1024 * ho:1024 * (ho + 1)]))
                tyu = dma("y", lambda e, ho=ho, par=par: e.dma_start(
                    out=wbuf_u[:, 1024 * par:1024 * (par + 1)],
                    in_=wu_sw.ap()[:, 1024 * ho:1024 * (ho + 1)]))
                wait("t", tyg)
                wait("t", tyu)
                if ho >= 1:
                    wait("t", tok_h[ho - 1][0])
                for c in range(DC):
                    if ho == 0:
                        wait("t", xn_toks[c])
                    tt_g = op("t", lambda e, c=c, par=par: e.matmul(
                        pA[:, :], wbuf_g[:, 1024 * par + 128 * c:1024 * par + 128 * (c + 1)],
                        xn[:, TT * c:TT * (c + 1)],
                        start=(c == 0), stop=(c == DC - 1), skip_group_check=True),
                            1 if c == DC - 1 else 0)
                if ho >= 1:
                    wait("t", tok_h[ho - 1][1])
                for c in range(DC):
                    tt_u = op("t", lambda e, c=c, par=par: e.matmul(
                        pB[:, :], wbuf_u[:, 1024 * par + 128 * c:1024 * par + 128 * (c + 1)],
                        xn[:, TT * c:TT * (c + 1)],
                        start=(c == 0), stop=(c == DC - 1), skip_group_check=True),
                            1 if c == DC - 1 else 0)
                st8["wgu_free"][par] = tt_u
                wait("s", tt_g)
                ts = op("s", lambda e, ho=ho: e.activation(
                    tmp[:, 512 * (ho % 2):512 * (ho % 2 + 1)], pA[:, :], AF.Silu), 1)
                wait("v", ts)
                wait("v", tt_u)
                tv = op("v", lambda e, ho=ho: e.tensor_mul(
                    h_sb[:, TT * ho:TT * (ho + 1)],
                    tmp[:, 512 * (ho % 2):512 * (ho % 2 + 1)], pB[:, :]), 1)
                tok_h.append((ts, tv))
            tok_hall = tok_h[HO - 1][1]
            tok_dn = [None, None]
            for do in range(DC):
                par = do % 2
                wait("y", st8["wd_free"][par])
                ty = dma("y", lambda e, do=do, par=par: e.dma_start(
                    out=wd_sb[:, HID * par:HID * (par + 1)],
                    in_=wd_sw.ap()[:, HID * do:HID * (do + 1)]))
                wait("t", ty)
                wait("t", tok_hall)
                if tok_dn[par] is not None:
                    wait("t", tok_dn[par])
                bank = pC if par == 0 else pD
                for hc in range(HO):
                    tt = op("t", lambda e, hc=hc, par=par, bank=bank: e.matmul(
                        bank[:, :], wd_sb[:, HID * par + 128 * hc:HID * par + 128 * (hc + 1)],
                        h_sb[:, TT * hc:TT * (hc + 1)],
                        start=(hc == 0), stop=(hc == HO - 1), skip_group_check=True),
                            1 if hc == HO - 1 else 0)
                st8["wd_free"][par] = tt
                wait("v", tt)
                tok_dn[par] = op("v", lambda e, do=do, bank=bank, xb=xb: e.tensor_add(
                    xb[:, TT * do:TT * (do + 1)], xb[:, TT * do:TT * (do + 1)],
                    bank[:, :]), 1)
            return op("v", lambda e: e.engine_nop(), 1)

        xt_readers = {0: [tok_vbias], 1: list(dequant_done)}
        tok_upd_t = tok("t")
        pend_xn1 = None
        bounce_qkv = []
        for sl in range(NF):
            own = sl >= NO
            j = sl - NO
            par = sl % 2
            xb = x_ts[par]
            if sl == 0:
                for tk in xt_readers[0]:
                    wait("y", tk)
                xt_readers[0] = []
                ty0 = dma("y", lambda e: e.dma_start(
                    out=x_tA.ap().rearrange("p (c t) -> p c t", c=DC),
                    in_=xt[:, 0:TT].rearrange("(c p) t -> p c t", p=128)))
                for _t in tok_consts:
                    wait("s", _t)
                wait("s", tok_vbias)
                pend_xn1 = ln_pass(xb, y2b, xn, pfree=tok_vbias, pre_s=[ty0])[1]
            tok_x2 = ffn_pass(xb, pend_xn1)
            bt_x2 = None
            if own:
                wait("g", tok_x2)
                bt_x2 = dma("g", lambda e, j=j, xb=xb: e.dma_start(
                    out=bx2.ap()[:, TT * j:TT * (j + 1)].rearrange("(c p) t -> p c t", p=128),
                    in_=xb.ap().rearrange("p (c t) -> p c t", c=DC)))
            tok_xn2, _, y2_toks, tok_ug = ln_pass(xb, gn, xn, extra="ln2",
                                                  pfree=tok_x2, pre_s=[tok_x2])
            gs_toks = []
            for go in range(DC):
                gpar = go % 2
                wait("y", st8["wo_free"][gpar])
                ty = dma("y", lambda e, go=go, gpar=gpar: e.dma_start(
                    out=wo_sb[:, 1024 * gpar:1024 * (gpar + 1)],
                    in_=gate_sw.ap()[:, 1024 * go:1024 * (go + 1)]))
                wait("t", ty)
                if go == 1:
                    wait("t", tok_ug)
                bank = pA if gpar == 0 else pB
                if go >= 2:
                    wait("t", gs_toks[go - 2])
                for c in range(DC):
                    if go == 0:
                        wait("t", y2_toks[c])
                    tt = op("t", lambda e, c=c, gpar=gpar, bank=bank: e.matmul(
                        bank[:, :], wo_sb[:, 1024 * gpar + 128 * c:1024 * gpar + 128 * (c + 1)],
                        y2b[:, TT * c:TT * (c + 1)],
                        start=(c == 0), stop=(c == DC - 1), skip_group_check=True),
                            1 if c == DC - 1 else 0)
                st8["wo_free"][gpar] = tt
                wait("s", tt)
                if go == 0:
                    wait("s", tok_xn2)
                ts2 = op("s", lambda e, go=go, bank=bank: e.activation(
                    tmp[:, TT * go:TT * (go + 1)], bank[:, :], AF.Sigmoid,
                    bias=gate_bc[:, go:go + 1]), 1)
                gs_toks.append(ts2)
            wait("t", gs_toks[-1])
            wait("t", tok_xn2)
            for c in range(DC):
                tt = op("t", lambda e, c=c: e.matmul(
                    pC[0:1, :], onescol[:, :], tmp[:, TT * c:TT * (c + 1)],
                    start=(c == 0), stop=(c == DC - 1), skip_group_check=True),
                        1 if c == DC - 1 else 0)
            wait("v", tt)
            op("v", lambda e: e.tensor_scalar(rows_f[0:1, 0:TT], pC[0:1, :],
                                              1.0 / D, EPS, ALU.mult, ALU.add))
            tv = op("v", lambda e: e.reciprocal(rows_f[0:1, TT:2 * TT],
                                                rows_f[0:1, 0:TT]), 1)
            wait("v", tv)
            tv = op("v", lambda e: e.tensor_copy(rows_b[0:1, 3 * TT:4 * TT],
                                                 rows_f[0:1, TT:2 * TT]), 1)
            wait("t", tv)
            tt = op("t", lambda e: e.matmul(pD[:, :], ones1r[:, :],
                                            rows_b[0:1, 3 * TT:4 * TT],
                                            start=True, stop=True,
                                            skip_group_check=True), 1)
            wait("v", tt)
            for c in range(DC):
                op("v", lambda e, c=c: e.tensor_mul(
                    gn[:, TT * c:TT * (c + 1)], tmp[:, TT * c:TT * (c + 1)], pD[:, :]))
            tok_gn = op("v", lambda e: e.engine_nop(), 1)

            def qkpart(offs, dest, tok_prev):
                toks = []
                for i in range(DC):
                    o = offs + i
                    qpar = o % 2
                    wait("y", st8["wo_free"][qpar])
                    ty = dma("y", lambda e, o=o, qpar=qpar: e.dma_start(
                        out=wo_sb[:, 1024 * qpar:1024 * (qpar + 1)],
                        in_=qk_sw.ap()[:, 1024 * o:1024 * (o + 1)]))
                    wait("t", ty)
                    if i == 0:
                        wait("t", tok_prev)
                    bank = pA if qpar == 0 else pB
                    if i >= 2:
                        wait("t", toks[i - 2])
                    for c in range(DC):
                        tt = op("t", lambda e, c=c, qpar=qpar, bank=bank: e.matmul(
                            bank[:, :],
                            wo_sb[:, 1024 * qpar + 128 * c:1024 * qpar + 128 * (c + 1)],
                            xn[:, TT * c:TT * (c + 1)],
                            start=(c == 0), stop=(c == DC - 1), skip_group_check=True),
                                1 if c == DC - 1 else 0)
                    st8["wo_free"][qpar] = tt
                    wait("v", tt)
                    if i == 0:
                        wait("v", tok_gn)
                        wait("v", st8["bq"] if offs == 0 else st8["bk"])
                    tv = op("v", lambda e, i=i, o=o, bank=bank, d=dest: e.scalar_tensor_tensor(
                        d[:, TT * i:TT * (i + 1)], bank[:, :], qkv_bc[:, o:o + 1],
                        gn[:, TT * i:TT * (i + 1)], ALU.add, ALU.mult), 1)
                    toks.append(tv)
                return toks

            def elu_gate(scr_buf, dest, toks_g):
                out_toks = []
                for c in range(DC):
                    wait("s", toks_g[c])
                    ts = op("s", lambda e, c=c, sb2=scr_buf, d=dest: e.activation(
                        sb2[:, TT * c:TT * (c + 1)], d[:, TT * c:TT * (c + 1)], AF.Exp), 1)
                    op("v", lambda e, c=c, d=dest: e.tensor_scalar_max(
                        h_sb[:, TT * c:TT * (c + 1)], d[:, TT * c:TT * (c + 1)], 0.0))
                    wait("v", ts)
                    tv2 = op("v", lambda e, c=c, sb2=scr_buf, d=dest: e.scalar_tensor_tensor(
                        d[:, TT * c:TT * (c + 1)], sb2[:, TT * c:TT * (c + 1)], 1.0,
                        h_sb[:, TT * c:TT * (c + 1)], ALU.min, ALU.add), 1)
                    out_toks.append(tv2)
                return out_toks

            bounce_qkv = []
            if own:
                q_raw = qkpart(0, Q, tok_gn)
                q_toks = elu_gate(tmp, Q, q_raw)
                wait("g", q_toks[-1])
                st8["bq"] = dma("g", lambda e, j=j: e.dma_start(
                    out=bq.ap()[:, TT * j:TT * (j + 1)].rearrange("(c p) t -> p c t", p=128),
                    in_=Q.ap().rearrange("p (c t) -> p c t", c=DC)))
                bounce_qkv.append(st8["bq"])
                kprev = q_toks[-1]
            else:
                kprev = tok_gn
            k_raw = qkpart(8, K, kprev)
            k_toks = elu_gate(y2b, K, k_raw)
            tok_k = k_toks[-1]
            if own:
                wait("g", tok_k)
                st8["bk"] = dma("g", lambda e, j=j: e.dma_start(
                    out=bk.ap()[:, TT * j:TT * (j + 1)].rearrange("(c p) t -> p c t", p=128),
                    in_=K.ap().rearrange("p (c t) -> p c t", c=DC)))
                bounce_qkv.append(st8["bk"])

            wait("y", st8["wv_free"])
            tyv = dma("y", lambda e: e.dma_start(out=wv_sb[:, :], in_=v_sw.ap()))
            tok_va = []
            for tc in range(NCH):
                b0 = pA if tc % 2 == 0 else pC
                b1 = pB if tc % 2 == 0 else pD
                wait("t", tyv)
                if tc == 0:
                    wait("t", tok_xn2)
                    wait("t", tok_k)
                if tc >= 2:
                    wait("t", tok_va[tc - 2])
                for c in range(DC):
                    op("t", lambda e, c=c, tc=tc, b0=b0: e.matmul(
                        b0[:, :], xn[:, TT * c + C * tc:TT * c + C * (tc + 1)],
                        wv_sb[:, D * c:D * c + 512],
                        start=(c == 0), stop=(c == DC - 1), skip_group_check=True))
                    tt = op("t", lambda e, c=c, tc=tc, b1=b1: e.matmul(
                        b1[:, :], xn[:, TT * c + C * tc:TT * c + C * (tc + 1)],
                        wv_sb[:, D * c + 512:D * (c + 1)],
                        start=(c == 0), stop=(c == DC - 1), skip_group_check=True),
                            1 if c == DC - 1 else 0)
                wait("v", tt)
                if tc == 0:
                    wait("v", st8["bva"])
                va4 = v_aug.ap().rearrange("p (tc h a) -> p tc h a", tc=NCH, h=H)
                op("v", lambda e, tc=tc, b0=b0, va4=va4: e.tensor_add(
                    va4[:, tc:tc + 1, 0:NP, 0:DH],
                    b0.ap().rearrange("p (o h i) -> p o h i", o=1, h=NP),
                    vbias[:, 0:512].rearrange("p (o h i) -> p o h i", o=1, h=NP)))
                tvv = op("v", lambda e, tc=tc, b1=b1, va4=va4: e.tensor_add(
                    va4[:, tc:tc + 1, NP:H, 0:DH],
                    b1.ap().rearrange("p (o h i) -> p o h i", o=1, h=NP),
                    vbias[:, 512:1024].rearrange("p (o h i) -> p o h i", o=1, h=NP)), 1)
                tok_va.append(tvv)
            st8["wv_free"] = tok("t")
            if own:
                wait("g", tok_va[-1])
                st8["bva"] = dma("g", lambda e, j=j: e.dma_start(
                    out=bva.ap()[:, NCH * SQW * j:NCH * SQW * (j + 1)], in_=v_aug[:, :]))
                bounce_qkv.append(st8["bva"])
                if DEBUG and sl == NO:
                    bounce_qkv.append(dma("g", lambda e, xb=xb: e.dma_start(
                        out=dbg["d_x2"], in_=xb[:, :])))
                    bounce_qkv.append(dma("g", lambda e: e.dma_start(out=dbg["d_xn2"], in_=xn[:, :])))
                    bounce_qkv.append(dma("g", lambda e: e.dma_start(out=dbg["d_q"], in_=Q[:, :])))
                    bounce_qkv.append(dma("g", lambda e: e.dma_start(out=dbg["d_k"], in_=K[:, :])))
                    bounce_qkv.append(dma("g", lambda e: e.dma_start(out=dbg["d_gn"], in_=gn[:, :])))
                    bounce_qkv.append(dma("g", lambda e: e.dma_start(out=dbg["d_va"], in_=v_aug[:, :])))

            ktoks = []
            tprev = None
            for tc in range(NCH):
                wait("t", tok_k)
                if tprev is not None:
                    wait("t", tprev)
                for p in range(NP):
                    tt = op("t", lambda e, p=p, tc=tc: e.transpose(
                        pT[:, 128 * p:128 * (p + 1)],
                        K[:, TT * p + C * tc:TT * p + C * (tc + 1)], ident[:, :]),
                            1 if p == NP - 1 else 0)
                wait("v", tt)
                tprev = op("v", lambda e, tc=tc: e.tensor_copy(
                    k_tm[:, NP * C * tc:NP * C * (tc + 1)], pT[:, :]), 1)
                ktoks.append(tprev)
            for tc in range(NCH):
                wait("t", ktoks[tc])
                wait("t", tok_va[tc])
                if sl == NO and tc == 0:
                    wait("t", st8["SrecEvac"])
                for p in range(NP):
                    stt = st8["upd_first"] and p in (0, 3, 6)
                    tok_upd_t = op("t", lambda e, p=p, tc=tc, stt=stt: e.matmul(
                        pS_blk(p), k_tm[:, NP * C * tc + 128 * p:NP * C * tc + 128 * (p + 1)],
                        v_aug[:, SQW * tc + PW * p:SQW * tc + PW * (p + 1)],
                        start=stt, stop=False, skip_group_check=True),
                            1 if p == NP - 1 else 0)
                st8["upd_first"] = False
            tok_ktm = tprev

            if sl == NO - 1:
                wait("v", tok_upd_t)
                for i, pt in enumerate(pS):
                    npp = 3 - (i == 2)
                    op("v", lambda e, i=i, pt=pt, npp=npp: e.tensor_copy(
                        s_rec[:, 390 * i:390 * i + npp * PW], pt[:, 0:npp * PW]))
                st8["SrecEvac"] = op("v", lambda e: e.engine_nop(), 1)
                st8["upd_first"] = True

            if sl + 1 < NF:
                npar = (sl + 1) % 2
                for tk in xt_readers[npar]:
                    wait("y", tk)
                xt_readers[npar] = []
                tyn = dma("y", lambda e, sl=sl, npar=npar: e.dma_start(
                    out=x_ts[npar].ap().rearrange("p (c t) -> p c t", c=DC),
                    in_=xt[:, TT * (sl + 1):TT * (sl + 2)].rearrange(
                        "(c p) t -> p c t", p=128)))
                pend_xn1 = ln_pass(x_ts[npar], y2b, xn, pfree=tok_va[-1],
                                   pre_s=[tyn, tok_k],
                                   out_ready=st8["wv_free"])[1]
            xt_readers[par] = [tok_xn2] + ([bt_x2] if bt_x2 else [])

        wait("v", tok_upd_t)
        for i, pt in enumerate(pS):
            npp = 3 - (i == 2)
            op("v", lambda e, i=i, pt=pt, npp=npp: e.tensor_copy(
                s_own[:, 390 * i:390 * i + npp * PW], pt[:, 0:npp * PW]))
        tok_SownEvac = op("v", lambda e: e.engine_nop(), 1)

        wait("g", tok_SownEvac)
        shv = shd.ap().rearrange("(a p) c -> a p c", p=256)
        te1 = dma("g", lambda e: e.dma_start(
            out=shv[bass.ts(st8["g"][0], 1), 0:128, :], in_=s_own[:, :]))
        te2 = dma("g", lambda e: e.dma_start(
            out=shv[bass.ts(st8["g"][0], 1), 128:256, :], in_=s_rec[:, :]))
        wait("g", te1)
        wait("g", te2)
        if cx.live == "g" and not SIM_LOCAL_SYNC:
            e = cx.eng
            par_, opar_, my_flag, pa_flag = st8["g"]
            nonce = e.value_load(nonce_i[0:1, 0:1])
            e.store(my_flag, nonce)

            def cond():
                return e.value_load(pa_flag) != nonce

            with e.While(cond):
                e.nop()
        rdix = 0 if SIM_LOCAL_SYNC else 1
        tp1 = dma("g", lambda e: e.dma_start(
            out=E1p[:, :], in_=shv[bass.ts(st8["g"][rdix], 1), 0:128, :]))
        tp2 = dma("g", lambda e: e.dma_start(
            out=E2p[:, :], in_=shv[bass.ts(st8["g"][rdix], 1), 128:256, :]))
        wait("v", tp1)
        wait("v", tp2)
        op("v", lambda e: e.tensor_scalar_mul(S_start[:, :], s_rec[:, :], mcols[:, 0:1]))
        op("v", lambda e: e.scalar_tensor_tensor(
            S_start[:, :], E2p[:, :], mcols[:, 1:2], S_start[:, :], ALU.mult, ALU.add))
        tok_sstart = op("v", lambda e: e.scalar_tensor_tensor(
            S_start[:, :], E1p[:, :], mcols[:, 2:3], S_start[:, :], ALU.mult, ALU.add), 1)
        if DEBUG:
            wait("g", tok_sstart)
            dma("g", lambda e: e.dma_start(out=dbg["d_sstart"], in_=S_start[:, :]))

        st8["upd_first"] = True
        tok_updprev = tok("t")
        tok_amcp = tok("v")
        tok_ssb = [None, None]
        lastmask = [None, None]
        ste = S_start[0:64, :].rearrange("p (pp a) -> p pp a", a=PW)
        sto = S_start[64:128, :].rearrange("p (pp a) -> p pp a", a=PW)

        def build_ssb(n, first):
            sb_ = S_sbs[n % 2]
            sse = sb_[0:64, :].rearrange("p (pp a) -> p pp a", a=PW)
            sso = sb_[64:128, :].rearrange("p (pp a) -> p pp a", a=PW)
            if first:
                wait("v", tok_sstart)
                op("v", lambda e, sse=sse: e.tensor_copy(sse[:, :, 0:AUG], ste[:, :, 0:AUG]))
                return op("v", lambda e, sso=sso: e.tensor_copy(
                    sso[:, :, AUG:PW], sto[:, :, AUG:PW]), 1)
            wait("v", tok_updprev)
            for i, pt in enumerate(pS):
                npp = 3 - (i == 2)
                pe4 = pt[:, 0:npp * PW].rearrange("p (pp a) -> p pp a", a=PW)
                op("v", lambda e, pe4=pe4, npp=npp, i=i, sse=sse: e.tensor_add(
                    sse[:, 3 * i:3 * i + npp, 0:AUG],
                    pe4[0:64, 0:npp, 0:AUG],
                    ste[:, 3 * i:3 * i + npp, 0:AUG]))
                op("v", lambda e, pe4=pe4, npp=npp, i=i, sso=sso: e.tensor_add(
                    sso[:, 3 * i:3 * i + npp, AUG:PW],
                    pe4[64:128, 0:npp, AUG:PW],
                    sto[:, 3 * i:3 * i + npp, AUG:PW]))
            return op("v", lambda e: e.engine_nop(), 1)

        def ktm_round(tok_kload, pt_free, tc2):
            wait("t", tok_kload)
            wait("t", pt_free)
            for p in range(NP):
                tt5 = op("t", lambda e, p=p, tc2=tc2: e.transpose(
                    pT[:, 128 * p:128 * (p + 1)],
                    K[:, TT * p + C * tc2:TT * p + C * (tc2 + 1)], ident[:, :]),
                        1 if p == NP - 1 else 0)
            wait("v", tt5)
            return op("v", lambda e, tc2=tc2: e.tensor_copy(
                k_tm[:, NP * C * tc2:NP * C * (tc2 + 1)], pT[:, :]), 1)

        def ktm_all(tok_kload, pt_free):
            tok_c = pt_free
            for tc2 in range(NCH):
                tok_c = ktm_round(tok_kload, tok_c, tc2)
            return tok_c

        wait("g", tok_upd_t)
        for tk in bounce_qkv:
            wait("g", tk)
        tok_ldq = dma("g", lambda e: e.dma_start(
            out=Q.ap().rearrange("p (c t) -> p c t", c=DC),
            in_=bq.ap()[:, 0:TT].rearrange("(c p) t -> p c t", p=128)))
        tok_ldk = dma("g", lambda e: e.dma_start(
            out=K.ap().rearrange("p (c t) -> p c t", c=DC),
            in_=bk.ap()[:, 0:TT].rearrange("(c p) t -> p c t", p=128)))
        tok_ldv = dma("g", lambda e: e.dma_start(
            out=v_aug[:, :], in_=bva.ap()[:, 0:NCH * SQW]))
        for tk in xt_readers[0]:
            wait("g", tk)
        xt_readers[0] = []
        tok_ldx = dma("g", lambda e: e.dma_start(
            out=x_tA.ap().rearrange("p (c t) -> p c t", c=DC),
            in_=bx2.ap()[:, 0:TT].rearrange("(c p) t -> p c t", p=128)))
        tok_ssb[0] = build_ssb(0, True)
        tok_ktm = ktm_all(tok_ldk, tok("v"))
        tok_out = tok("v")

        for j in range(NO):
            par = j % 2
            xb = x_ts[par]
            for tc in range(NCH):
                n = j * NCH + tc
                sp = n % 2
                if tc == 0:
                    wait("t", tok_ldv)
                    wait("t", tok_ktm)
                sb_cur = S_sbs[sp]

                def emit_outs(hh, tok_m):
                    p2 = hh // 2
                    wait("t", tok_m)
                    ob = pC if hh < 8 else pD
                    oreg = 64 * (hh % 8)
                    op("t", lambda e, hh=hh, tc=tc, ob=ob, oreg=oreg: e.matmul(
                        ob[:, oreg:oreg + DH], s_Am[:, C * (hh % 2):C * (hh % 2 + 1)],
                        v_aug[:, SQW * tc + AUG * hh:SQW * tc + AUG * hh + DH],
                        start=(hh in (0, 8)), stop=False, skip_group_check=True))
                    op("t", lambda e, hh=hh, tc=tc: e.matmul(
                        pS2[:, 400 + hh:401 + hh],
                        s_Am[:, C * (hh % 2):C * (hh % 2 + 1)],
                        v_aug[:, SQW * tc + AUG * hh + DH:SQW * tc + AUG * (hh + 1)],
                        start=False, stop=False, skip_group_check=True))
                    op("t", lambda e, hh=hh, p2=p2, tc=tc, ob=ob, oreg=oreg, sb_=sb_cur: e.matmul(
                        ob[:, oreg:oreg + DH], Q[:, TT * p2 + C * tc:TT * p2 + C * (tc + 1)],
                        sb_[:, AUG * hh:AUG * hh + DH],
                        start=False, stop=False, skip_group_check=True))
                    return op("t", lambda e, hh=hh, p2=p2, tc=tc, sb_=sb_cur: e.matmul(
                        pS2[:, 400 + hh:401 + hh],
                        Q[:, TT * p2 + C * tc:TT * p2 + C * (tc + 1)],
                        sb_[:, AUG * hh + DH:AUG * (hh + 1)],
                        start=False, stop=False, skip_group_check=True),
                        1 if hh == H - 1 else 0)

                masks = [None] * H
                for hh in range(H):
                    p2 = hh // 2
                    bb = 64 * (hh % 2)
                    bank = pA if hh % 2 == 0 else pB
                    if hh == 0:
                        wait("t", tok_ssb[sp])
                        wait("t", tok_den_free if n > 0 else tok_sstart)
                        wait("t", tok_ldq)
                        wait("t", tok_out)
                    if lastmask[hh % 2] is not None:
                        wait("t", lastmask[hh % 2])
                    tt = op("t", lambda e, p2=p2, bb=bb, tc=tc, bank=bank: e.matmul(
                        bank[:, 0:C], K[bb:bb + 64, TT * p2 + C * tc:TT * p2 + C * (tc + 1)],
                        Q[bb:bb + 64, TT * p2 + C * tc:TT * p2 + C * (tc + 1)],
                        start=True, stop=True, skip_group_check=True), 1)
                    wait("v", tt)
                    masks[hh] = op("v", lambda e, hh=hh, bank=bank: e.tensor_mul(
                        s_Am[:, C * (hh % 2):C * (hh % 2 + 1)], bank[:, 0:C],
                        mask_tri[:, :]), 1)
                    lastmask[hh % 2] = masks[hh]
                    if hh >= 1:
                        tok_heads = emit_outs(hh - 1, masks[hh - 1])
                tok_heads = emit_outs(H - 1, masks[H - 1])
                for p in range(6):
                    stt = st8["upd_first"] and p in (0, 3)
                    op("t", lambda e, p=p, tc=tc, stt=stt: e.matmul(
                        pS_blk(p), k_tm[:, NP * C * tc + 128 * p:NP * C * tc + 128 * (p + 1)],
                        v_aug[:, SQW * tc + PW * p:SQW * tc + PW * (p + 1)],
                        start=stt, stop=False, skip_group_check=True))
                wait("v", tok_heads)
                op("v", lambda e: e.tensor_scalar_add(den_sb[:, :], pS2[:, 400:416], EPS))
                tvd = op("v", lambda e: e.reciprocal(rd_sb[:, :], den_sb[:, :]), 1)
                wait("v", tvd)
                for hh in range(H):
                    ob = pC if hh < 8 else pD
                    oreg = 64 * (hh % 8)
                    op("v", lambda e, hh=hh, ob=ob, oreg=oreg: e.tensor_scalar_mul(
                        tmp[:, 64 * hh:64 * (hh + 1)], ob[:, oreg:oreg + DH],
                        rd_sb[:, hh:hh + 1]))
                tok_otm = op("v", lambda e: e.engine_nop(), 1)
                tok_den_free = op("v", lambda e: e.memset(pS2[:, 400:416], 0.0), 1)
                if DEBUG and j == 0 and tc == 0:
                    wait("g", tok_otm)
                    dma("g", lambda e: e.dma_start(out=dbg["d_otm"], in_=tmp[:, 0:1024]))
                wait("t", tok_otm)
                for p in range(NP):
                    tt2 = op("t", lambda e, p=p: e.transpose(
                        pT[:, 128 * p:128 * (p + 1)], tmp[:, 128 * p:128 * (p + 1)],
                        ident[:, :]), 1 if p == NP - 1 else 0)
                wait("v", tt2)
                tok_amcp = op("v", lambda e, tc=tc: e.tensor_copy(
                    gn.ap().rearrange("p (pp t) -> p pp t", pp=NP)[:, :, C * tc:C * (tc + 1)],
                    pT.ap().rearrange("p (pp t) -> p pp t", pp=NP)), 1)
                wait("t", tok_den_free)
                for p in range(6, NP):
                    stt = st8["upd_first"] and p == 6
                    tok_updprev = op("t", lambda e, p=p, tc=tc, stt=stt: e.matmul(
                        pS_blk(p), k_tm[:, NP * C * tc + 128 * p:NP * C * tc + 128 * (p + 1)],
                        v_aug[:, SQW * tc + PW * p:SQW * tc + PW * (p + 1)],
                        start=stt, stop=False, skip_group_check=True),
                            1 if p == NP - 1 else 0)
                st8["upd_first"] = False
                if n + 1 < NO * NCH:
                    tok_ssb[(n + 1) % 2] = build_ssb(n + 1, False)

            if j + 1 < NO:
                wait("g", tok_updprev)
                tok_ldk = dma("g", lambda e, j=j: e.dma_start(
                    out=K.ap().rearrange("p (c t) -> p c t", c=DC),
                    in_=bk.ap()[:, TT * (j + 1):TT * (j + 2)].rearrange(
                        "(c p) t -> p c t", p=128)))
                tok_ldq = dma("g", lambda e, j=j: e.dma_start(
                    out=Q.ap().rearrange("p (c t) -> p c t", c=DC),
                    in_=bq.ap()[:, TT * (j + 1):TT * (j + 2)].rearrange(
                        "(c p) t -> p c t", p=128)))
                tok_ldv = dma("g", lambda e, j=j: e.dma_start(
                    out=v_aug[:, :],
                    in_=bva.ap()[:, NCH * SQW * (j + 1):NCH * SQW * (j + 2)]))
                npar = (j + 1) % 2
                for tk in xt_readers[npar]:
                    wait("g", tk)
                xt_readers[npar] = []
                tok_ldx = dma("g", lambda e, j=j, npar=npar: e.dma_start(
                    out=x_ts[npar].ap().rearrange("p (c t) -> p c t", c=DC),
                    in_=bx2.ap()[:, TT * (j + 1):TT * (j + 2)].rearrange(
                        "(c p) t -> p c t", p=128)))

            tok_pj = [None, None]
            for po in range(DC):
                ppar = po % 2
                wait("y", st8["wo_free"][ppar])
                ty = dma("y", lambda e, po=po, ppar=ppar: e.dma_start(
                    out=wo_sb[:, 1024 * ppar:1024 * (ppar + 1)],
                    in_=proj_sw.ap()[:, 1024 * po:1024 * (po + 1)]))
                wait("t", ty)
                if po == 0:
                    wait("t", tok_amcp)
                    wait("t", tok_ldx)
                bank = pA if ppar == 0 else pB
                if tok_pj[ppar] is not None:
                    wait("t", tok_pj[ppar])
                for c in range(DC):
                    tt = op("t", lambda e, c=c, ppar=ppar, bank=bank: e.matmul(
                        bank[:, :], wo_sb[:, 1024 * ppar + 128 * c:1024 * ppar + 128 * (c + 1)],
                        gn[:, TT * c:TT * (c + 1)],
                        start=(c == 0), stop=(c == DC - 1), skip_group_check=True),
                            1 if c == DC - 1 else 0)
                st8["wo_free"][ppar] = tt
                wait("v", tt)
                tok_pj[ppar] = op("v", lambda e, po=po, bank=bank, xb=xb: e.scalar_tensor_tensor(
                    xb[:, TT * po:TT * (po + 1)], bank[:, :], proj_bc[:, po:po + 1],
                    xb[:, TT * po:TT * (po + 1)], ALU.add, ALU.add), 1)
                if 4 <= po < 4 + NCH and j + 1 < NO:
                    tok_ktm = ktm_round(tok_ldk, tok_amcp if po == 4 else tok_ktm,
                                        po - 4)
            tok_x3 = op("v", lambda e: e.engine_nop(), 1)
            tok_xn3s = ln_pass(xb, y2b, xn, pfree=tok_x3,
                               pre_s=[tok_x3, tok_amcp])[1]
            tok_out = ffn_pass(xb, tok_xn3s)
            wait("g", tok_out)
            _to = dma("g", lambda e, j=j, xb=xb: e.dma_start(
                out=out_t[:, TT * j:TT * (j + 1)].rearrange("(c p) t -> p c t", p=128),
                in_=xb.ap().rearrange("p (c t) -> p c t", c=DC)))
            xt_readers[par] = [_to, tok_out]


    sems = {}
    with ctxm:
        with ExitStack() as semstack:
            names = ["t", "v", "s"] + ["y%d" % i for i in range(NLANE)] + \
                ["g%d" % i for i in range(NLANE)]
            for n in names:
                sems[n] = semstack.enter_context(nc.semaphore("sem_" + n))
            with nc.Block() as block:
                @block.tensor
                def _(eng):
                    schedule(Ctx("t", eng, sems))

                @block.vector
                def _(eng):
                    schedule(Ctx("v", eng, sems))

                @block.scalar
                def _(eng):
                    schedule(Ctx("s", eng, sems))

                @block.sync
                def _(eng):
                    schedule(Ctx("y", eng, sems))

                @block.gpsimd
                def _(eng):
                    schedule(Ctx("g", eng, sems))

    return nc


def stage_inputs(inputs):
    x = np.ascontiguousarray(np.asarray(inputs["L0_attn_x"], np.float32))
    qkv_b = np.asarray(inputs["L0_attn_qkv_b"], np.float32)
    wg_i = np.ascontiguousarray(np.asarray(inputs["L0_ffn_w_gate"], np.int32).T)
    wu_i = np.ascontiguousarray(np.asarray(inputs["L0_ffn_w_up"], np.int32).T)
    wd_i = np.ascontiguousarray(np.asarray(inputs["L0_ffn_w_down"], np.int32).T)
    qkvT = np.ascontiguousarray(np.asarray(inputs["L0_attn_qkv_w"], np.float32).T)
    gateT = np.ascontiguousarray(np.asarray(inputs["L0_attn_gate_w"], np.float32).T)
    projT = np.ascontiguousarray(np.asarray(inputs["L0_attn_proj_w"], np.float32).T)

    def col(v, n):
        return np.ascontiguousarray(np.broadcast_to(v.reshape(n, 128).T, (128, n)))

    qkv_bc = col(qkv_b[0:2048], 16)
    vb_row = np.ascontiguousarray(qkv_b[2048:3072].reshape(1, D))
    gate_bc = col(np.asarray(inputs["L0_attn_gate_b"], np.float32), DC)
    proj_bc = col(np.asarray(inputs["L0_attn_proj_b"], np.float32), DC)
    ln_gc = col(np.asarray(inputs["L0_attn_ln_g"], np.float32), DC)
    ln_bc = col(np.asarray(inputs["L0_attn_ln_b"], np.float32), DC)
    mask_tri = np.triu(np.ones((C, C), np.float32))
    identm = np.eye(C).astype(ml_dtypes.bfloat16)
    masks = {0: (0., 0., 0.), 1: (0., 0., 1.), 2: (1., 1., 0.), 3: (1., 1., 1.)}
    consts = np.ascontiguousarray(
        np.broadcast_to(np.array([EPS, -0.5], np.float32), (128, 2)))
    import os as _os
    nonce = (int.from_bytes(_os.urandom(4), "little") & 0x7FFFFFFF) | 1
    nonce_arr = np.full((1, 16), nonce, np.int32)

    in_maps = []
    OWN = T // 4
    for c in range(8):
        b, r = c // 4, c % 4
        own = x[b, r * OWN:(r + 1) * OWN]
        if r == 2:
            rec = x[b, 0:OWN]
        elif r == 3:
            rec = x[b, OWN:2 * OWN]
        else:
            rec = np.zeros_like(own)
        xt = np.ascontiguousarray(np.concatenate([rec, own], axis=0).T)
        mc = np.ascontiguousarray(
            np.broadcast_to(np.array(masks[r], np.float32), (128, 3)))
        in_maps.append({
            "xt": xt, "wgT_i": wg_i, "wuT_i": wu_i, "wdT_i": wd_i,
            "qkvT": qkvT, "gateT": gateT, "projT": projT,
            "qkv_bc": qkv_bc, "vb_row": vb_row, "gate_bc": gate_bc,
            "proj_bc": proj_bc, "ln_gc": ln_gc, "ln_bc": ln_bc,
            "mcols": mc, "mask_tri": mask_tri, "ident": identm,
            "consts_i": consts, "nonce_i": nonce_arr,
        })
    return in_maps


def kernel(**inputs):
    in_maps = stage_inputs(inputs)
    nc = build_program()
    res = run_bass_kernel_spmd(nc, in_maps, core_ids=list(range(8)))
    out = np.zeros((B, T, D), np.float32)
    OWN = T // 4
    for c in range(8):
        b, r = c // 4, c % 4
        out[b, r * OWN:(r + 1) * OWN] = res.results[c]["out_t"].T
    return out


# version 08
# speedup vs baseline: 1.1358x; 1.1358x over previous
import numpy as np
import ml_dtypes
import concourse.bass as bass
import concourse.mybir as mybir
from concourse.bass_utils import run_bass_kernel_spmd
from contextlib import ExitStack

F32 = mybir.dt.float32
BF16 = mybir.dt.bfloat16
I32 = mybir.dt.int32
ALU = mybir.AluOpType
AF = mybir.ActivationFunctionType
EPS = 1e-5

B, T, D = 2, 16384, 1024
H, DH, C = 16, 64, 128
HID = 3072
DC = D // 128
HO = HID // 128
TT = 512
NCH = TT // C
NF = 16
NO = 8
NP = 8
AUG = DH + 1
PW = 2 * AUG
SQW = NP * PW

DEBUG = False
RACE_DETECT = True
SIM_LOCAL_SYNC = False


NLANE = 4


class Ctx:

    def __init__(self, live, eng, sems):
        self.live = live
        self.eng = eng
        self.sems = sems
        self.c = {k: 0 for k in sems}
        self.lane_i = {"y": 0, "g": 0}

    def op(self, eng, fn, inc=0, sem=None, chain=True):
        if sem is None:
            sem = eng
        chain = chain and eng in ("v", "s") and sem == eng
        if chain:
            if self.live == eng and self.c[eng] > 0:
                self.eng.wait_ge(self.sems[eng], self.c[eng])
            inc = 1
        if self.live == eng:
            inst = fn(self.eng)
            if inc:
                inst.then_inc(self.sems[sem], inc)
        if inc:
            self.c[sem] += inc
        return (sem, self.c[sem])

    def dma(self, eng, fn):
        lane = "%s%d" % (eng, self.lane_i[eng] % NLANE)
        self.lane_i[eng] += 1
        self.wait(eng, (lane, self.c[lane]))
        return self.op(eng, fn, 16, sem=lane)

    def wait(self, eng, tok):
        if tok is None:
            return
        sem, val = tok
        if val <= 0:
            return
        if self.live == eng:
            self.eng.wait_ge(self.sems[sem], val)

    def tok(self, sem):
        return (sem, self.c[sem])


def build_program():
    nc = bass.Bass("TRN2", target_bir_lowering=False,
                   detect_race_conditions=RACE_DETECT)

    xt = nc.dram_tensor("xt", [D, NF * TT], F32, kind="ExternalInput").ap()
    wgT_i = nc.dram_tensor("wgT_i", [D, HID], I32, kind="ExternalInput").ap()
    wuT_i = nc.dram_tensor("wuT_i", [D, HID], I32, kind="ExternalInput").ap()
    wdT_i = nc.dram_tensor("wdT_i", [HID, D], I32, kind="ExternalInput").ap()
    qkvT_i = nc.dram_tensor("qkvT", [D, 3 * D], F32, kind="ExternalInput").ap()
    gateT_i = nc.dram_tensor("gateT", [D, D], F32, kind="ExternalInput").ap()
    projT_i = nc.dram_tensor("projT", [D, D], F32, kind="ExternalInput").ap()
    qkv_bc_i = nc.dram_tensor("qkv_bc", [128, 16], F32, kind="ExternalInput").ap()
    vb_row_i = nc.dram_tensor("vb_row", [1, D], F32, kind="ExternalInput").ap()
    gate_bc_i = nc.dram_tensor("gate_bc", [128, DC], F32, kind="ExternalInput").ap()
    proj_bc_i = nc.dram_tensor("proj_bc", [128, DC], F32, kind="ExternalInput").ap()
    ln_gc_i = nc.dram_tensor("ln_gc", [128, DC], F32, kind="ExternalInput").ap()
    ln_bc_i = nc.dram_tensor("ln_bc", [128, DC], F32, kind="ExternalInput").ap()
    mcols_i = nc.dram_tensor("mcols", [128, 3], F32, kind="ExternalInput").ap()
    mask_i = nc.dram_tensor("mask_tri", [C, C], F32, kind="ExternalInput").ap()
    consts_i = nc.dram_tensor("consts_i", [128, 2], F32, kind="ExternalInput").ap()
    nonce_i = nc.dram_tensor("nonce_i", [1, 16], I32, kind="ExternalInput").ap()
    ident_i = nc.dram_tensor("ident", [C, C], BF16, kind="ExternalInput").ap()
    out_t = nc.dram_tensor("out_t", [D, NO * TT], F32, kind="ExternalOutput").ap()

    wg_sw = nc.dram_tensor("wg_sw", [128, HO * 1024], BF16)
    wu_sw = nc.dram_tensor("wu_sw", [128, HO * 1024], BF16)
    wd_sw = nc.dram_tensor("wd_sw", [128, DC * HID], BF16)
    qk_sw = nc.dram_tensor("qk_sw", [128, 16 * 1024], BF16)
    v_sw = nc.dram_tensor("v_sw", [128, DC * D], BF16)
    gate_sw = nc.dram_tensor("gate_sw", [128, DC * 1024], BF16)
    proj_sw = nc.dram_tensor("proj_sw", [128, DC * 1024], BF16)
    bx2 = nc.dram_tensor("bx2", [D, NO * TT], F32)
    bq = nc.dram_tensor("bq", [D, NO * TT], BF16)
    bk = nc.dram_tensor("bk", [D, NO * TT], BF16)
    bva = nc.dram_tensor("bva", [128, NO * NCH * SQW], BF16)
    shd = nc.dram_tensor("shd", [512, SQW], F32, addr_space="Shared")
    shf = nc.dram_tensor("shf", [2, 16], I32, addr_space="Shared")

    ctxm = ExitStack()
    sbuf = lambda name, shape, dt: ctxm.enter_context(nc.sbuf_tensor(name, shape, dt))
    psum = lambda name, shape, dt: ctxm.enter_context(nc.psum_tensor(name, shape, dt))

    x_tA = sbuf("x_tA", [128, DC * TT], F32)
    xn = sbuf("xn", [128, DC * TT], BF16)
    tmp = sbuf("tmp", [128, DC * TT], BF16)
    y2b = sbuf("y2b", [128, DC * TT], BF16)
    gn = sbuf("gn", [128, DC * TT], BF16)
    Q = sbuf("Q", [128, DC * TT], BF16)
    K = sbuf("K", [128, DC * TT], BF16)
    v_aug = sbuf("v_aug", [128, NCH * SQW], BF16)
    h_sb = sbuf("h_sb", [128, HO * TT], BF16)
    k_tm = sbuf("k_tm", [128, NCH * NP * C], BF16)
    s_Am = sbuf("s_Am", [128, 2 * C], BF16)
    S_sb0 = sbuf("S_sb0", [128, H * AUG], BF16)
    S_sb1 = sbuf("S_sb1", [128, H * AUG], BF16)
    S_sbs = [S_sb0, S_sb1]
    S_start = sbuf("S_start", [128, SQW], F32)
    s_rec = sbuf("s_rec", [128, SQW], F32)
    s_own = sbuf("s_own", [128, SQW], F32)
    E2p = sbuf("E2p", [128, SQW], F32)
    E1p = s_own
    rows_f = sbuf("rows_f", [1, 5 * TT], F32)
    rows_b = sbuf("rows_b", [1, 4 * TT], BF16)
    wbuf_g = sbuf("wbuf_g", [128, 3 * 1024], BF16)
    wbuf_u = sbuf("wbuf_u", [128, 3 * 1024], BF16)
    wd_sb = sbuf("wd_sb", [128, 2 * HID], BF16)
    wv_sb = sbuf("wv_sb", [128, DC * D], BF16)
    wo_sb = sbuf("wo_sb", [128, 2 * 1024], BF16)
    wint = sbuf("wint", [128, HID], I32)
    wcast = sbuf("wcast", [128, HID], BF16)
    _wint_off = nc.lookup_mloc(wint).addr
    x_tB = nc.alloc_sbuf_tensor_at("x_tB", [128, DC * TT], F32, offset=_wint_off)
    x_ts = [x_tA, x_tB]
    qkv_bc = sbuf("qkv_bcs", [128, 16], F32)
    gate_bc = sbuf("gate_bcs", [128, DC], F32)
    proj_bc = sbuf("proj_bcs", [128, DC], F32)
    ln_gc = sbuf("ln_gcs", [128, DC], F32)
    ln_bc = sbuf("ln_bcs", [128, DC], F32)
    mcols = sbuf("mcolss", [128, 3], F32)
    vbias = sbuf("vbias", [128, D], F32)
    mask_tri = sbuf("mask_tris", [C, C], F32)
    ident = sbuf("idents", [C, C], BF16)
    onescol = sbuf("onescol", [128, 1], BF16)
    negcol = sbuf("negcol", [128, 1], BF16)
    ones1r = sbuf("ones1r", [1, 128], BF16)
    consts_sb = sbuf("consts_sb", [128, 2], F32)
    nc.const_aps.register(F32, EPS, consts_sb[:, 0:1])
    nc.const_aps.register(F32, -0.5, consts_sb[:, 1:2])
    den_sb = sbuf("den_sb", [128, 16], F32)
    rd_sb = sbuf("rd_sb", [128, 16], F32)

    pA = psum("pA", [128, 512], F32)
    pB = psum("pB", [128, 512], F32)
    pC = psum("pC", [128, 512], F32)
    pD = psum("pD", [128, 512], F32)
    pT = psum("pT", [128, 1024], BF16)
    pS0 = psum("pS0", [128, 3 * PW], F32)
    pS1 = psum("pS1", [128, 3 * PW], F32)
    pS2 = psum("pS2", [128, 512], F32)
    pS = [pS0, pS1, pS2]

    def pS_blk(p):
        return pS[p // 3][:, (p % 3) * PW:(p % 3) * PW + PW]

    dbg = {}
    if DEBUG:
        dbg["d_x2"] = nc.dram_tensor("d_x2", [128, DC * TT], F32, kind="ExternalOutput").ap()
        for nme in ["d_xn2", "d_q", "d_k", "d_gn"]:
            dbg[nme] = nc.dram_tensor(nme, [128, DC * TT], BF16, kind="ExternalOutput").ap()
        dbg["d_va"] = nc.dram_tensor("d_va", [128, NCH * SQW], BF16, kind="ExternalOutput").ap()
        dbg["d_sstart"] = nc.dram_tensor("d_sstart", [128, SQW], F32, kind="ExternalOutput").ap()
        dbg["d_otm"] = nc.dram_tensor("d_otm", [128, 1024], BF16, kind="ExternalOutput").ap()

    def schedule(cx: Ctx):
        op, wait, tok = cx.op, cx.wait, cx.tok
        dma = cx.dma
        st8 = {"upd_first": True, "SrecEvac": None, "wv_free": None,
               "bq": None, "bk": None, "bva": None,
               "wgu_free": [None, None], "wd_free": [None, None],
               "wo_free": [None, None]}

        if cx.live == "g":
            e = cx.eng
            cid = e.partition_id()
            par = cid % 2
            opar = (cid + 1) % 2
            my_flag = shf.ap()[bass.ts(par, 1), 0:1]
            pa_flag = shf.ap()[bass.ts(opar, 1), 0:1]
            st8["g"] = (par, opar, my_flag, pa_flag)

        _ct = [
            dma("y", lambda e: e.dma_start(out=mask_tri[:, :], in_=mask_i)),
            dma("y", lambda e: e.dma_start(out=ident[:, :], in_=ident_i)),
            dma("y", lambda e: e.dma_start(out=qkv_bc[:, :], in_=qkv_bc_i)),
            dma("y", lambda e: e.dma_start(out=gate_bc[:, :], in_=gate_bc_i)),
            dma("y", lambda e: e.dma_start(out=proj_bc[:, :], in_=proj_bc_i)),
            dma("y", lambda e: e.dma_start(out=ln_gc[:, :], in_=ln_gc_i)),
            dma("y", lambda e: e.dma_start(out=ln_bc[:, :], in_=ln_bc_i)),
            dma("y", lambda e: e.dma_start(out=mcols[:, :], in_=mcols_i)),
            dma("y", lambda e: e.dma_start(out=consts_sb[:, :], in_=consts_i)),
        ]
        tok_consts = _ct[-NLANE:]
        op("v", lambda e: e.memset(onescol[:, :], 1.0))
        op("v", lambda e: e.memset(negcol[:, :], -1.0))
        op("v", lambda e: e.memset(ones1r[:, :], 1.0))
        op("v", lambda e: e.memset(v_aug[:, :], 1.0))
        op("v", lambda e: e.memset(S_sb0[:, :], 0.0), 1)
        tok_init_v = op("v", lambda e: e.memset(S_sb1[:, :], 0.0), 1)

        tv = None
        tg = None
        for wi, wsw, scl, cols, pat in (
                (wgT_i, wg_sw, 1.0 / 127.0, HID, "g"),
                (wuT_i, wu_sw, 1.0 / 127.0, HID, "g"),
                (wdT_i, wd_sw, 1.0 / 254.0, D, "d")):
            nchunks = DC if cols == HID else HO
            for c in range(nchunks):
                if tv is not None:
                    wait("y", tv)
                tk = dma("y", lambda e, wi=wi, c=c, cols=cols: e.dma_start(
                    out=wint[:, 0:cols], in_=wi[128 * c:128 * (c + 1), :]))
                wait("v", tk)
                if tg is not None:
                    wait("v", tg)
                tv = op("v", lambda e, scl=scl, cols=cols: e.tensor_scalar_mul(
                    wcast[:, 0:cols], wint[:, 0:cols], scl), 1)
                wait("g", tv)
                if pat == "g":
                    tg = dma("g", lambda e, wsw=wsw, c=c: e.dma_start(
                        out=wsw.ap().rearrange("p (ho c i) -> p ho c i",
                                               ho=HO, c=DC)[:, :, c, :],
                        in_=wcast.ap().rearrange("p (ho i) -> p ho i", ho=HO)))
                else:
                    tg = dma("g", lambda e, c=c: e.dma_start(
                        out=wd_sw.ap().rearrange("p (do hc i) -> p do hc i",
                                                 do=DC, hc=HO)[:, :, c, :],
                        in_=wcast[:, 0:D].rearrange("p (do i) -> p do i", do=DC)))
        lasty = []
        for c in range(DC):
            wait("g", tv)
            wait("g", tg)
            for _t in lasty:
                wait("g", _t)
            tg = dma("g", lambda e, c=c: e.dma_start(
                out=wcast[:, :], in_=qkvT_i[128 * c:128 * (c + 1), :]))
            wait("y", tg)
            _q1 = dma("y", lambda e, c=c: e.dma_start(
                out=qk_sw.ap().rearrange("p (o c i) -> p o c i", o=16, c=DC)[:, :, c, :],
                in_=wcast[:, 0:2048].rearrange("p (o i) -> p o i", o=16)))
            _q2 = dma("y", lambda e, c=c: e.dma_start(
                out=v_sw.ap()[:, D * c:D * (c + 1)], in_=wcast[:, 2048:3072]))
            lasty = [_q1, _q2]
        for src, dst in ((gateT_i, gate_sw), (projT_i, proj_sw)):
            for c in range(DC):
                for _t in lasty:
                    wait("g", _t)
                tg = dma("g", lambda e, src=src, c=c: e.dma_start(
                    out=wcast[:, 0:D], in_=src[128 * c:128 * (c + 1), :]))
                wait("y", tg)
                lasty = [dma("y", lambda e, dst=dst, c=c: e.dma_start(
                    out=dst.ap().rearrange("p (o c i) -> p o c i", o=DC, c=DC)[:, :, c, :],
                    in_=wcast[:, 0:D].rearrange("p (o i) -> p o i", o=DC)))]
        for _t in lasty:
            wait("g", _t)
        tg = dma("g", lambda e: e.dma_start(out=wcast[0:1, 0:D], in_=vb_row_i))
        wait("t", tg)
        wait("t", tok_init_v)
        tvb = None
        for half in range(2):
            if tvb is not None:
                wait("t", tvb)
            tth = op("t", lambda e, half=half: e.matmul(
                pA[:, :], ones1r[:, :], wcast[0:1, 512 * half:512 * (half + 1)],
                start=True, stop=True, skip_group_check=True), 1)
            wait("v", tth)
            tvb = op("v", lambda e, half=half: e.tensor_copy(
                vbias[:, 512 * half:512 * (half + 1)], pA[:, :]), 1)
        tok_vbias = tvb
        dequant_done = [tok_vbias, tok("t")] + list(lasty)

        st8["wv_free"] = tok("t")
        st8["wgu_free"] = [tok("t"), tok("t"), tok("t")]
        st8["wd_free"] = [tok("t"), tok("t")]
        st8["wo_free"] = [tok("t"), tok("t")]

        def ln_pass(xb, sq_scratch, out_buf, extra=None, pfree=None,
                    pre_s=(), out_ready=None):
            R = lambda i: rows_f[0:1, TT * i:TT * (i + 1)]
            RB = lambda i: rows_b[0:1, TT * i:TT * (i + 1)]
            for tk in pre_s:
                wait("s", tk)
            ts1 = op("s", lambda e: e.activation(tmp[:, :], xb[:, :], AF.Copy), 1)
            ts2 = op("s", lambda e, b=sq_scratch: e.activation(
                b[:, :], tmp[:, :], AF.Square), 1)
            wait("t", ts1)
            wait("t", pfree)
            for c in range(DC):
                op("t", lambda e, c=c: e.matmul(
                    pC[0:1, :], negcol[:, :], tmp[:, TT * c:TT * (c + 1)],
                    start=(c == 0), stop=(c == DC - 1), skip_group_check=True))
            wait("t", ts2)
            for c in range(DC):
                tt = op("t", lambda e, c=c, b=sq_scratch: e.matmul(
                    pD[0:1, :], onescol[:, :], b[:, TT * c:TT * (c + 1)],
                    start=(c == 0), stop=(c == DC - 1), skip_group_check=True),
                        1 if c == DC - 1 else 0)
            wait("v", tt)
            tvr = op("v", lambda e: e.tensor_scalar_mul(RB(0), pC[0:1, :], 1.0 / D), 1)
            op("v", lambda e: e.tensor_mul(R(2), RB(0), RB(0)))
            tvv2 = op("v", lambda e: e.scalar_tensor_tensor(
                R(3), pD[0:1, :], 1.0 / D, R(2), ALU.mult, ALU.subtract), 1)
            wait("s", tvv2)
            op("s", lambda e: e.activation(R(4), R(3), AF.Ln, bias=EPS))
            ts3 = op("s", lambda e: e.activation(RB(1), R(4), AF.Exp, scale=-0.5), 1)
            if extra == "ln2":
                op("v", lambda e: e.tensor_scalar_add(R(1), R(3), EPS))
                tv2 = op("v", lambda e: e.reciprocal(R(2), R(1)), 1)
                tv2b = op("v", lambda e: e.tensor_mul(R(1), R(3), R(2)), 1)
                wait("s", tv2b)
                op("s", lambda e: e.activation(R(2), R(1), AF.Ln, bias=EPS))
                ts4 = op("s", lambda e: e.activation(R(1), R(2), AF.Exp, scale=-0.5), 1)
                wait("v", ts4)
                tva = op("v", lambda e: e.tensor_mul(RB(2), RB(1), R(1)), 1)
            wait("t", tvv2)
            if extra == "ln2":
                wait("t", tv2b)
            tt2 = op("t", lambda e: e.matmul(pD[:, :], ones1r[:, :], RB(0),
                                             start=True, stop=True,
                                             skip_group_check=True))
            wait("t", ts3)
            tt3 = op("t", lambda e: e.matmul(pC[:, :], ones1r[:, :], RB(1),
                                             start=True, stop=True,
                                             skip_group_check=True),
                     0 if extra == "ln2" else 1)
            if extra == "ln2":
                wait("t", tva)
                tt3 = op("t", lambda e: e.matmul(pB[:, :], ones1r[:, :], RB(2),
                                                 start=True, stop=True,
                                                 skip_group_check=True), 1)
            wait("v", tt3)
            if out_ready is not None:
                wait("v", out_ready)
            chunk_toks = []
            y2_toks = []
            if extra == "ln2":
                for c in range(DC):
                    op("v", lambda e, c=c: e.tensor_add(
                        tmp[:, TT * c:TT * (c + 1)], xb[:, TT * c:TT * (c + 1)],
                        pD[:, :]))
                    y2_toks.append(op("v", lambda e, c=c: e.tensor_mul(
                        y2b[:, TT * c:TT * (c + 1)], tmp[:, TT * c:TT * (c + 1)],
                        pC[:, :]), 1))
                ug = None
                for c in range(DC):
                    op("v", lambda e, c=c: e.tensor_mul(
                        gn[:, TT * c:TT * (c + 1)], tmp[:, TT * c:TT * (c + 1)],
                        pB[:, :]))
                    if c == DC - 1:
                        ug = op("v", lambda e: e.engine_nop(), 1)
                    chunk_toks.append(op("v", lambda e, c=c, ob=out_buf: e.tensor_scalar(
                        ob[:, TT * c:TT * (c + 1)], gn[:, TT * c:TT * (c + 1)],
                        ln_gc[:, c:c + 1], ln_bc[:, c:c + 1], ALU.mult, ALU.add), 1))
                return chunk_toks[-1], chunk_toks, y2_toks, ug
            else:
                for c in range(DC):
                    op("v", lambda e, c=c: e.tensor_add(
                        tmp[:, TT * c:TT * (c + 1)], xb[:, TT * c:TT * (c + 1)],
                        pD[:, :]))
                    chunk_toks.append(op("v", lambda e, c=c, ob=out_buf: e.tensor_mul(
                        ob[:, TT * c:TT * (c + 1)], tmp[:, TT * c:TT * (c + 1)],
                        pC[:, :]), 1))
                return chunk_toks[-1], chunk_toks, None, None

        def ffn_pass(xb, xn_toks):
            tok_h = []
            for ho in range(HO):
                par = ho % 3
                wait("y", st8["wgu_free"][par])
                tyg = dma("y", lambda e, ho=ho, par=par: e.dma_start(
                    out=wbuf_g[:, 1024 * par:1024 * (par + 1)],
                    in_=wg_sw.ap()[:, 1024 * ho:1024 * (ho + 1)]))
                tyu = dma("y", lambda e, ho=ho, par=par: e.dma_start(
                    out=wbuf_u[:, 1024 * par:1024 * (par + 1)],
                    in_=wu_sw.ap()[:, 1024 * ho:1024 * (ho + 1)]))
                wait("t", tyg)
                wait("t", tyu)
                gbank = pA if ho % 2 == 0 else pC
                ubank = pB if ho % 2 == 0 else pD
                if ho >= 2:
                    wait("t", tok_h[ho - 2][0])
                for c in range(DC):
                    if ho == 0:
                        wait("t", xn_toks[c])
                    tt_g = op("t", lambda e, c=c, par=par, gbank=gbank: e.matmul(
                        gbank[:, :], wbuf_g[:, 1024 * par + 128 * c:1024 * par + 128 * (c + 1)],
                        xn[:, TT * c:TT * (c + 1)],
                        start=(c == 0), stop=(c == DC - 1), skip_group_check=True),
                            1 if c == DC - 1 else 0)
                if ho >= 2:
                    wait("t", tok_h[ho - 2][1])
                for c in range(DC):
                    tt_u = op("t", lambda e, c=c, par=par, ubank=ubank: e.matmul(
                        ubank[:, :], wbuf_u[:, 1024 * par + 128 * c:1024 * par + 128 * (c + 1)],
                        xn[:, TT * c:TT * (c + 1)],
                        start=(c == 0), stop=(c == DC - 1), skip_group_check=True),
                            1 if c == DC - 1 else 0)
                st8["wgu_free"][par] = tt_u
                wait("s", tt_g)
                if ho >= 2:
                    wait("s", tok_h[ho - 2][1])
                ts = op("s", lambda e, ho=ho, gbank=gbank: e.activation(
                    tmp[:, 512 * (ho % 2):512 * (ho % 2 + 1)], gbank[:, :], AF.Silu), 1)
                wait("v", ts)
                wait("v", tt_u)
                tv = op("v", lambda e, ho=ho, ubank=ubank: e.tensor_mul(
                    h_sb[:, TT * ho:TT * (ho + 1)],
                    tmp[:, 512 * (ho % 2):512 * (ho % 2 + 1)], ubank[:, :]), 1)
                tok_h.append((ts, tv))
            tok_hall = tok_h[HO - 1][1]
            tok_dn = [None, None]
            for do in range(DC):
                par = do % 2
                wait("y", st8["wd_free"][par])
                ty = dma("y", lambda e, do=do, par=par: e.dma_start(
                    out=wd_sb[:, HID * par:HID * (par + 1)],
                    in_=wd_sw.ap()[:, HID * do:HID * (do + 1)]))
                wait("t", ty)
                wait("t", tok_hall)
                if do == 0:
                    wait("t", tok_h[HO - 1][0])
                if do == 1:
                    wait("t", tok_h[HO - 2][1])
                if tok_dn[par] is not None:
                    wait("t", tok_dn[par])
                bank = pC if par == 0 else pD
                for hc in range(HO):
                    tt = op("t", lambda e, hc=hc, par=par, bank=bank: e.matmul(
                        bank[:, :], wd_sb[:, HID * par + 128 * hc:HID * par + 128 * (hc + 1)],
                        h_sb[:, TT * hc:TT * (hc + 1)],
                        start=(hc == 0), stop=(hc == HO - 1), skip_group_check=True),
                            1 if hc == HO - 1 else 0)
                st8["wd_free"][par] = tt
                wait("v", tt)
                tok_dn[par] = op("v", lambda e, do=do, bank=bank, xb=xb: e.tensor_add(
                    xb[:, TT * do:TT * (do + 1)], xb[:, TT * do:TT * (do + 1)],
                    bank[:, :]), 1)
            return op("v", lambda e: e.engine_nop(), 1)

        xt_readers = {0: [tok_vbias], 1: list(dequant_done)}
        tok_upd_t = tok("t")
        pend_xn1 = None
        bounce_qkv = []
        for sl in range(NF):
            own = sl >= NO
            j = sl - NO
            par = sl % 2
            xb = x_ts[par]
            if sl == 0:
                for tk in xt_readers[0]:
                    wait("y", tk)
                xt_readers[0] = []
                ty0 = dma("y", lambda e: e.dma_start(
                    out=x_tA.ap().rearrange("p (c t) -> p c t", c=DC),
                    in_=xt[:, 0:TT].rearrange("(c p) t -> p c t", p=128)))
                for _t in tok_consts:
                    wait("s", _t)
                wait("s", tok_vbias)
                pend_xn1 = ln_pass(xb, y2b, xn, pfree=tok_vbias, pre_s=[ty0])[1]
            tok_x2 = ffn_pass(xb, pend_xn1)
            bt_x2 = None
            if own:
                wait("g", tok_x2)
                bt_x2 = dma("g", lambda e, j=j, xb=xb: e.dma_start(
                    out=bx2.ap()[:, TT * j:TT * (j + 1)].rearrange("(c p) t -> p c t", p=128),
                    in_=xb.ap().rearrange("p (c t) -> p c t", c=DC)))
            tok_xn2, _, y2_toks, tok_ug = ln_pass(xb, gn, xn, extra="ln2",
                                                  pfree=tok_x2, pre_s=[tok_x2])
            gs_toks = []
            for go in range(DC):
                gpar = go % 2
                wait("y", st8["wo_free"][gpar])
                ty = dma("y", lambda e, go=go, gpar=gpar: e.dma_start(
                    out=wo_sb[:, 1024 * gpar:1024 * (gpar + 1)],
                    in_=gate_sw.ap()[:, 1024 * go:1024 * (go + 1)]))
                wait("t", ty)
                if go == 1:
                    wait("t", tok_ug)
                bank = pA if gpar == 0 else pB
                if go >= 2:
                    wait("t", gs_toks[go - 2])
                for c in range(DC):
                    if go == 0:
                        wait("t", y2_toks[c])
                    tt = op("t", lambda e, c=c, gpar=gpar, bank=bank: e.matmul(
                        bank[:, :], wo_sb[:, 1024 * gpar + 128 * c:1024 * gpar + 128 * (c + 1)],
                        y2b[:, TT * c:TT * (c + 1)],
                        start=(c == 0), stop=(c == DC - 1), skip_group_check=True),
                            1 if c == DC - 1 else 0)
                st8["wo_free"][gpar] = tt
                wait("s", tt)
                if go == 0:
                    wait("s", tok_xn2)
                ts2 = op("s", lambda e, go=go, bank=bank: e.activation(
                    tmp[:, TT * go:TT * (go + 1)], bank[:, :], AF.Sigmoid,
                    bias=gate_bc[:, go:go + 1]), 1)
                gs_toks.append(ts2)
            wait("t", gs_toks[-1])
            wait("t", tok_xn2)
            for c in range(DC):
                tt = op("t", lambda e, c=c: e.matmul(
                    pC[0:1, :], onescol[:, :], tmp[:, TT * c:TT * (c + 1)],
                    start=(c == 0), stop=(c == DC - 1), skip_group_check=True),
                        1 if c == DC - 1 else 0)
            wait("v", tt)
            op("v", lambda e: e.tensor_scalar(rows_f[0:1, 0:TT], pC[0:1, :],
                                              1.0 / D, EPS, ALU.mult, ALU.add))
            tv = op("v", lambda e: e.reciprocal(rows_f[0:1, TT:2 * TT],
                                                rows_f[0:1, 0:TT]), 1)
            wait("v", tv)
            tv = op("v", lambda e: e.tensor_copy(rows_b[0:1, 3 * TT:4 * TT],
                                                 rows_f[0:1, TT:2 * TT]), 1)
            wait("t", tv)
            tt = op("t", lambda e: e.matmul(pD[:, :], ones1r[:, :],
                                            rows_b[0:1, 3 * TT:4 * TT],
                                            start=True, stop=True,
                                            skip_group_check=True), 1)
            wait("v", tt)
            for c in range(DC):
                op("v", lambda e, c=c: e.tensor_mul(
                    gn[:, TT * c:TT * (c + 1)], tmp[:, TT * c:TT * (c + 1)], pD[:, :]))
            tok_gn = op("v", lambda e: e.engine_nop(), 1)

            def qkpart(offs, dest, tok_prev):
                toks = []
                for i in range(DC):
                    o = offs + i
                    qpar = o % 2
                    wait("y", st8["wo_free"][qpar])
                    ty = dma("y", lambda e, o=o, qpar=qpar: e.dma_start(
                        out=wo_sb[:, 1024 * qpar:1024 * (qpar + 1)],
                        in_=qk_sw.ap()[:, 1024 * o:1024 * (o + 1)]))
                    wait("t", ty)
                    if i == 0:
                        wait("t", tok_prev)
                    bank = pA if qpar == 0 else pB
                    if i >= 2:
                        wait("t", toks[i - 2])
                    for c in range(DC):
                        tt = op("t", lambda e, c=c, qpar=qpar, bank=bank: e.matmul(
                            bank[:, :],
                            wo_sb[:, 1024 * qpar + 128 * c:1024 * qpar + 128 * (c + 1)],
                            xn[:, TT * c:TT * (c + 1)],
                            start=(c == 0), stop=(c == DC - 1), skip_group_check=True),
                                1 if c == DC - 1 else 0)
                    st8["wo_free"][qpar] = tt
                    wait("v", tt)
                    if i == 0:
                        wait("v", tok_gn)
                        wait("v", st8["bq"] if offs == 0 else st8["bk"])
                    tv = op("v", lambda e, i=i, o=o, bank=bank, d=dest: e.scalar_tensor_tensor(
                        d[:, TT * i:TT * (i + 1)], bank[:, :], qkv_bc[:, o:o + 1],
                        gn[:, TT * i:TT * (i + 1)], ALU.add, ALU.mult), 1)
                    toks.append(tv)
                return toks

            def elu_gate(scr_buf, dest, toks_g):
                out_toks = []
                for c in range(DC):
                    wait("s", toks_g[c])
                    ts = op("s", lambda e, c=c, sb2=scr_buf, d=dest: e.activation(
                        sb2[:, TT * c:TT * (c + 1)], d[:, TT * c:TT * (c + 1)], AF.Exp), 1)
                    op("v", lambda e, c=c, d=dest: e.tensor_scalar_max(
                        h_sb[:, TT * c:TT * (c + 1)], d[:, TT * c:TT * (c + 1)], 0.0))
                    wait("v", ts)
                    tv2 = op("v", lambda e, c=c, sb2=scr_buf, d=dest: e.scalar_tensor_tensor(
                        d[:, TT * c:TT * (c + 1)], sb2[:, TT * c:TT * (c + 1)], 1.0,
                        h_sb[:, TT * c:TT * (c + 1)], ALU.min, ALU.add), 1)
                    out_toks.append(tv2)
                return out_toks

            bounce_qkv = []
            if own:
                q_raw = qkpart(0, Q, tok_gn)
                q_toks = elu_gate(tmp, Q, q_raw)
                wait("g", q_toks[-1])
                st8["bq"] = dma("g", lambda e, j=j: e.dma_start(
                    out=bq.ap()[:, TT * j:TT * (j + 1)].rearrange("(c p) t -> p c t", p=128),
                    in_=Q.ap().rearrange("p (c t) -> p c t", c=DC)))
                bounce_qkv.append(st8["bq"])
                kprev = q_toks[-1]
            else:
                kprev = tok_gn
            k_raw = qkpart(8, K, kprev)
            k_toks = elu_gate(y2b, K, k_raw)
            tok_k = k_toks[-1]
            if own:
                wait("g", tok_k)
                st8["bk"] = dma("g", lambda e, j=j: e.dma_start(
                    out=bk.ap()[:, TT * j:TT * (j + 1)].rearrange("(c p) t -> p c t", p=128),
                    in_=K.ap().rearrange("p (c t) -> p c t", c=DC)))
                bounce_qkv.append(st8["bk"])

            wait("y", st8["wv_free"])
            tyv = dma("y", lambda e: e.dma_start(out=wv_sb[:, :], in_=v_sw.ap()))
            tok_va = []
            for tc in range(NCH):
                b0 = pA if tc % 2 == 0 else pC
                b1 = pB if tc % 2 == 0 else pD
                wait("t", tyv)
                if tc == 0:
                    wait("t", tok_xn2)
                    wait("t", tok_k)
                if tc >= 2:
                    wait("t", tok_va[tc - 2])
                for c in range(DC):
                    op("t", lambda e, c=c, tc=tc, b0=b0: e.matmul(
                        b0[:, :], xn[:, TT * c + C * tc:TT * c + C * (tc + 1)],
                        wv_sb[:, D * c:D * c + 512],
                        start=(c == 0), stop=(c == DC - 1), skip_group_check=True))
                    tt = op("t", lambda e, c=c, tc=tc, b1=b1: e.matmul(
                        b1[:, :], xn[:, TT * c + C * tc:TT * c + C * (tc + 1)],
                        wv_sb[:, D * c + 512:D * (c + 1)],
                        start=(c == 0), stop=(c == DC - 1), skip_group_check=True),
                            1 if c == DC - 1 else 0)
                wait("v", tt)
                if tc == 0:
                    wait("v", st8["bva"])
                va4 = v_aug.ap().rearrange("p (tc h a) -> p tc h a", tc=NCH, h=H)
                op("v", lambda e, tc=tc, b0=b0, va4=va4: e.tensor_add(
                    va4[:, tc:tc + 1, 0:NP, 0:DH],
                    b0.ap().rearrange("p (o h i) -> p o h i", o=1, h=NP),
                    vbias[:, 0:512].rearrange("p (o h i) -> p o h i", o=1, h=NP)))
                tvv = op("v", lambda e, tc=tc, b1=b1, va4=va4: e.tensor_add(
                    va4[:, tc:tc + 1, NP:H, 0:DH],
                    b1.ap().rearrange("p (o h i) -> p o h i", o=1, h=NP),
                    vbias[:, 512:1024].rearrange("p (o h i) -> p o h i", o=1, h=NP)), 1)
                tok_va.append(tvv)
            st8["wv_free"] = tok("t")
            if own:
                wait("g", tok_va[-1])
                st8["bva"] = dma("g", lambda e, j=j: e.dma_start(
                    out=bva.ap()[:, NCH * SQW * j:NCH * SQW * (j + 1)], in_=v_aug[:, :]))
                bounce_qkv.append(st8["bva"])
                if DEBUG and sl == NO:
                    bounce_qkv.append(dma("g", lambda e, xb=xb: e.dma_start(
                        out=dbg["d_x2"], in_=xb[:, :])))
                    bounce_qkv.append(dma("g", lambda e: e.dma_start(out=dbg["d_xn2"], in_=xn[:, :])))
                    bounce_qkv.append(dma("g", lambda e: e.dma_start(out=dbg["d_q"], in_=Q[:, :])))
                    bounce_qkv.append(dma("g", lambda e: e.dma_start(out=dbg["d_k"], in_=K[:, :])))
                    bounce_qkv.append(dma("g", lambda e: e.dma_start(out=dbg["d_gn"], in_=gn[:, :])))
                    bounce_qkv.append(dma("g", lambda e: e.dma_start(out=dbg["d_va"], in_=v_aug[:, :])))

            if sl + 1 < NF:
                npar = (sl + 1) % 2
                for tk in xt_readers[npar]:
                    wait("y", tk)
                xt_readers[npar] = []
                tyn = dma("y", lambda e, sl=sl, npar=npar: e.dma_start(
                    out=x_ts[npar].ap().rearrange("p (c t) -> p c t", c=DC),
                    in_=xt[:, TT * (sl + 1):TT * (sl + 2)].rearrange(
                        "(c p) t -> p c t", p=128)))
                pend_xn1 = ln_pass(x_ts[npar], y2b, xn, pfree=tok_va[-1],
                                   pre_s=[tyn, tok_k],
                                   out_ready=st8["wv_free"])[1]

            ktoks = []
            tprev = None
            for tc in range(NCH):
                wait("t", tok_k)
                if tprev is not None:
                    wait("t", tprev)
                for p in range(NP):
                    tt = op("t", lambda e, p=p, tc=tc: e.transpose(
                        pT[:, 128 * p:128 * (p + 1)],
                        K[:, TT * p + C * tc:TT * p + C * (tc + 1)], ident[:, :]),
                            1 if p == NP - 1 else 0)
                wait("v", tt)
                tprev = op("v", lambda e, tc=tc: e.tensor_copy(
                    k_tm[:, NP * C * tc:NP * C * (tc + 1)], pT[:, :]), 1)
                ktoks.append(tprev)
            for tc in range(NCH):
                wait("t", ktoks[tc])
                wait("t", tok_va[tc])
                if sl == NO and tc == 0:
                    wait("t", st8["SrecEvac"])
                for p in range(NP):
                    stt = st8["upd_first"] and p in (0, 3, 6)
                    tok_upd_t = op("t", lambda e, p=p, tc=tc, stt=stt: e.matmul(
                        pS_blk(p), k_tm[:, NP * C * tc + 128 * p:NP * C * tc + 128 * (p + 1)],
                        v_aug[:, SQW * tc + PW * p:SQW * tc + PW * (p + 1)],
                        start=stt, stop=False, skip_group_check=True),
                            1 if p == NP - 1 else 0)
                st8["upd_first"] = False
            tok_ktm = tprev

            if sl == NO - 1:
                wait("v", tok_upd_t)
                for i, pt in enumerate(pS):
                    npp = 3 - (i == 2)
                    op("v", lambda e, i=i, pt=pt, npp=npp: e.tensor_copy(
                        s_rec[:, 390 * i:390 * i + npp * PW], pt[:, 0:npp * PW]))
                st8["SrecEvac"] = op("v", lambda e: e.engine_nop(), 1)
                st8["upd_first"] = True

            xt_readers[par] = [tok_xn2] + ([bt_x2] if bt_x2 else [])

        wait("v", tok_upd_t)
        for i, pt in enumerate(pS):
            npp = 3 - (i == 2)
            op("v", lambda e, i=i, pt=pt, npp=npp: e.tensor_copy(
                s_own[:, 390 * i:390 * i + npp * PW], pt[:, 0:npp * PW]))
        tok_SownEvac = op("v", lambda e: e.engine_nop(), 1)

        wait("g", tok_SownEvac)
        shv = shd.ap().rearrange("(a p) c -> a p c", p=256)
        te1 = dma("g", lambda e: e.dma_start(
            out=shv[bass.ts(st8["g"][0], 1), 0:128, :], in_=s_own[:, :]))
        te2 = dma("g", lambda e: e.dma_start(
            out=shv[bass.ts(st8["g"][0], 1), 128:256, :], in_=s_rec[:, :]))
        wait("g", te1)
        wait("g", te2)
        if cx.live == "g" and not SIM_LOCAL_SYNC:
            e = cx.eng
            par_, opar_, my_flag, pa_flag = st8["g"]
            nonce = e.value_load(nonce_i[0:1, 0:1])
            e.store(my_flag, nonce)

            def cond():
                return e.value_load(pa_flag) != nonce

            with e.While(cond):
                e.nop()
        rdix = 0 if SIM_LOCAL_SYNC else 1
        tp1 = dma("g", lambda e: e.dma_start(
            out=E1p[:, :], in_=shv[bass.ts(st8["g"][rdix], 1), 0:128, :]))
        tp2 = dma("g", lambda e: e.dma_start(
            out=E2p[:, :], in_=shv[bass.ts(st8["g"][rdix], 1), 128:256, :]))
        wait("v", tp1)
        wait("v", tp2)
        op("v", lambda e: e.tensor_scalar_mul(S_start[:, :], s_rec[:, :], mcols[:, 0:1]))
        op("v", lambda e: e.scalar_tensor_tensor(
            S_start[:, :], E2p[:, :], mcols[:, 1:2], S_start[:, :], ALU.mult, ALU.add))
        tok_sstart = op("v", lambda e: e.scalar_tensor_tensor(
            S_start[:, :], E1p[:, :], mcols[:, 2:3], S_start[:, :], ALU.mult, ALU.add), 1)
        if DEBUG:
            wait("g", tok_sstart)
            dma("g", lambda e: e.dma_start(out=dbg["d_sstart"], in_=S_start[:, :]))

        st8["upd_first"] = True
        tok_updprev = tok("t")
        tok_amcp = tok("v")
        tok_ssb = [None, None]
        lastmask = [None, None]
        ste = S_start[0:64, :].rearrange("p (pp a) -> p pp a", a=PW)
        sto = S_start[64:128, :].rearrange("p (pp a) -> p pp a", a=PW)

        def build_ssb(n, first):
            sb_ = S_sbs[n % 2]
            sse = sb_[0:64, :].rearrange("p (pp a) -> p pp a", a=PW)
            sso = sb_[64:128, :].rearrange("p (pp a) -> p pp a", a=PW)
            if first:
                wait("v", tok_sstart)
                op("v", lambda e, sse=sse: e.tensor_copy(sse[:, :, 0:AUG], ste[:, :, 0:AUG]))
                return op("v", lambda e, sso=sso: e.tensor_copy(
                    sso[:, :, AUG:PW], sto[:, :, AUG:PW]), 1)
            wait("v", tok_updprev)
            for i, pt in enumerate(pS):
                npp = 3 - (i == 2)
                pe4 = pt[:, 0:npp * PW].rearrange("p (pp a) -> p pp a", a=PW)
                op("v", lambda e, pe4=pe4, npp=npp, i=i, sse=sse: e.tensor_add(
                    sse[:, 3 * i:3 * i + npp, 0:AUG],
                    pe4[0:64, 0:npp, 0:AUG],
                    ste[:, 3 * i:3 * i + npp, 0:AUG]), 1, chain=(i == 0))
                op("v", lambda e, pe4=pe4, npp=npp, i=i, sso=sso: e.tensor_add(
                    sso[:, 3 * i:3 * i + npp, AUG:PW],
                    pe4[64:128, 0:npp, AUG:PW],
                    sto[:, 3 * i:3 * i + npp, AUG:PW]), 1, chain=False)
            return op("v", lambda e: e.engine_nop(), 1)

        def ktm_round(tok_kload, pt_free, tc2):
            wait("t", tok_kload)
            wait("t", pt_free)
            for p in range(NP):
                tt5 = op("t", lambda e, p=p, tc2=tc2: e.transpose(
                    pT[:, 128 * p:128 * (p + 1)],
                    K[:, TT * p + C * tc2:TT * p + C * (tc2 + 1)], ident[:, :]),
                        1 if p == NP - 1 else 0)
            wait("v", tt5)
            return op("v", lambda e, tc2=tc2: e.tensor_copy(
                k_tm[:, NP * C * tc2:NP * C * (tc2 + 1)], pT[:, :]), 1)

        def ktm_all(tok_kload, pt_free):
            tok_c = pt_free
            for tc2 in range(NCH):
                tok_c = ktm_round(tok_kload, tok_c, tc2)
            return tok_c

        wait("g", tok_upd_t)
        for tk in bounce_qkv:
            wait("g", tk)
        tok_ldq = dma("g", lambda e: e.dma_start(
            out=Q.ap().rearrange("p (c t) -> p c t", c=DC),
            in_=bq.ap()[:, 0:TT].rearrange("(c p) t -> p c t", p=128)))
        tok_ldk = dma("g", lambda e: e.dma_start(
            out=K.ap().rearrange("p (c t) -> p c t", c=DC),
            in_=bk.ap()[:, 0:TT].rearrange("(c p) t -> p c t", p=128)))
        tok_ldv = dma("g", lambda e: e.dma_start(
            out=v_aug[:, :], in_=bva.ap()[:, 0:NCH * SQW]))
        for tk in xt_readers[0]:
            wait("g", tk)
        xt_readers[0] = []
        tok_ldx = dma("g", lambda e: e.dma_start(
            out=x_tA.ap().rearrange("p (c t) -> p c t", c=DC),
            in_=bx2.ap()[:, 0:TT].rearrange("(c p) t -> p c t", p=128)))
        tok_ssb[0] = build_ssb(0, True)
        tok_ktm = ktm_all(tok_ldk, tok("v"))
        tok_out = tok("v")

        for j in range(NO):
            par = j % 2
            xb = x_ts[par]
            for tc in range(NCH):
                n = j * NCH + tc
                sp = n % 2
                if tc == 0:
                    wait("t", tok_ldv)
                    wait("t", tok_ktm)
                sb_cur = S_sbs[sp]

                def emit_outs(hh, tok_m):
                    p2 = hh // 2
                    wait("t", tok_m)
                    ob = pC if hh < 8 else pD
                    oreg = 64 * (hh % 8)
                    op("t", lambda e, hh=hh, tc=tc, ob=ob, oreg=oreg: e.matmul(
                        ob[:, oreg:oreg + DH], s_Am[:, C * (hh % 2):C * (hh % 2 + 1)],
                        v_aug[:, SQW * tc + AUG * hh:SQW * tc + AUG * hh + DH],
                        start=(hh in (0, 8)), stop=False, skip_group_check=True))
                    op("t", lambda e, hh=hh, tc=tc: e.matmul(
                        pS2[:, 400 + hh:401 + hh],
                        s_Am[:, C * (hh % 2):C * (hh % 2 + 1)],
                        v_aug[:, SQW * tc + AUG * hh + DH:SQW * tc + AUG * (hh + 1)],
                        start=False, stop=False, skip_group_check=True))
                    op("t", lambda e, hh=hh, p2=p2, tc=tc, ob=ob, oreg=oreg, sb_=sb_cur: e.matmul(
                        ob[:, oreg:oreg + DH], Q[:, TT * p2 + C * tc:TT * p2 + C * (tc + 1)],
                        sb_[:, AUG * hh:AUG * hh + DH],
                        start=False, stop=False, skip_group_check=True))
                    return op("t", lambda e, hh=hh, p2=p2, tc=tc, sb_=sb_cur: e.matmul(
                        pS2[:, 400 + hh:401 + hh],
                        Q[:, TT * p2 + C * tc:TT * p2 + C * (tc + 1)],
                        sb_[:, AUG * hh + DH:AUG * (hh + 1)],
                        start=False, stop=False, skip_group_check=True),
                        1 if hh == H - 1 else 0)

                masks = [None] * H
                for hh in range(H):
                    p2 = hh // 2
                    bb = 64 * (hh % 2)
                    bank = pA if hh % 2 == 0 else pB
                    if hh == 0:
                        wait("t", tok_ssb[sp])
                        wait("t", tok_den_free if n > 0 else tok_sstart)
                        wait("t", tok_ldq)
                        wait("t", tok_out)
                    if lastmask[hh % 2] is not None:
                        wait("t", lastmask[hh % 2])
                    tt = op("t", lambda e, p2=p2, bb=bb, tc=tc, bank=bank: e.matmul(
                        bank[:, 0:C], K[bb:bb + 64, TT * p2 + C * tc:TT * p2 + C * (tc + 1)],
                        Q[bb:bb + 64, TT * p2 + C * tc:TT * p2 + C * (tc + 1)],
                        start=True, stop=True, skip_group_check=True), 1)
                    wait("v", tt)
                    masks[hh] = op("v", lambda e, hh=hh, bank=bank: e.tensor_mul(
                        s_Am[:, C * (hh % 2):C * (hh % 2 + 1)], bank[:, 0:C],
                        mask_tri[:, :]), 1)
                    lastmask[hh % 2] = masks[hh]
                    if hh >= 1:
                        tok_heads = emit_outs(hh - 1, masks[hh - 1])
                tok_heads = emit_outs(H - 1, masks[H - 1])
                for p in range(6):
                    stt = st8["upd_first"] and p in (0, 3)
                    op("t", lambda e, p=p, tc=tc, stt=stt: e.matmul(
                        pS_blk(p), k_tm[:, NP * C * tc + 128 * p:NP * C * tc + 128 * (p + 1)],
                        v_aug[:, SQW * tc + PW * p:SQW * tc + PW * (p + 1)],
                        start=stt, stop=False, skip_group_check=True))
                wait("v", tok_heads)
                op("v", lambda e: e.tensor_scalar_add(den_sb[:, :], pS2[:, 400:416], EPS))
                tvd = op("v", lambda e: e.reciprocal(rd_sb[:, :], den_sb[:, :]), 1)
                wait("v", tvd)
                for hh in range(H):
                    ob = pC if hh < 8 else pD
                    oreg = 64 * (hh % 8)
                    op("v", lambda e, hh=hh, ob=ob, oreg=oreg: e.tensor_scalar_mul(
                        tmp[:, 64 * hh:64 * (hh + 1)], ob[:, oreg:oreg + DH],
                        rd_sb[:, hh:hh + 1]), 1, chain=(hh == 0))
                tok_otm = op("v", lambda e: e.engine_nop(), 1)
                tok_den_free = op("v", lambda e: e.memset(pS2[:, 400:416], 0.0), 1)
                if DEBUG and j == 0 and tc == 0:
                    wait("g", tok_otm)
                    dma("g", lambda e: e.dma_start(out=dbg["d_otm"], in_=tmp[:, 0:1024]))
                wait("t", tok_den_free)
                for p in range(6, NP):
                    stt = st8["upd_first"] and p == 6
                    tok_updprev = op("t", lambda e, p=p, tc=tc, stt=stt: e.matmul(
                        pS_blk(p), k_tm[:, NP * C * tc + 128 * p:NP * C * tc + 128 * (p + 1)],
                        v_aug[:, SQW * tc + PW * p:SQW * tc + PW * (p + 1)],
                        start=stt, stop=False, skip_group_check=True),
                            1 if p == NP - 1 else 0)
                st8["upd_first"] = False
                if n + 1 < NO * NCH:
                    tok_ssb[(n + 1) % 2] = build_ssb(n + 1, False)
                wait("t", tok_otm)
                for p in range(NP):
                    tt2 = op("t", lambda e, p=p: e.transpose(
                        pT[:, 128 * p:128 * (p + 1)], tmp[:, 128 * p:128 * (p + 1)],
                        ident[:, :]), 1 if p == NP - 1 else 0)
                wait("v", tt2)
                tok_amcp = op("v", lambda e, tc=tc: e.tensor_copy(
                    gn.ap().rearrange("p (pp t) -> p pp t", pp=NP)[:, :, C * tc:C * (tc + 1)],
                    pT.ap().rearrange("p (pp t) -> p pp t", pp=NP)), 1)

            if j + 1 < NO:
                wait("g", tok_updprev)
                tok_ldk = dma("g", lambda e, j=j: e.dma_start(
                    out=K.ap().rearrange("p (c t) -> p c t", c=DC),
                    in_=bk.ap()[:, TT * (j + 1):TT * (j + 2)].rearrange(
                        "(c p) t -> p c t", p=128)))
                tok_ldq = dma("g", lambda e, j=j: e.dma_start(
                    out=Q.ap().rearrange("p (c t) -> p c t", c=DC),
                    in_=bq.ap()[:, TT * (j + 1):TT * (j + 2)].rearrange(
                        "(c p) t -> p c t", p=128)))
                tok_ldv = dma("g", lambda e, j=j: e.dma_start(
                    out=v_aug[:, :],
                    in_=bva.ap()[:, NCH * SQW * (j + 1):NCH * SQW * (j + 2)]))
                npar = (j + 1) % 2
                for tk in xt_readers[npar]:
                    wait("g", tk)
                xt_readers[npar] = []
                tok_ldx = dma("g", lambda e, j=j, npar=npar: e.dma_start(
                    out=x_ts[npar].ap().rearrange("p (c t) -> p c t", c=DC),
                    in_=bx2.ap()[:, TT * (j + 1):TT * (j + 2)].rearrange(
                        "(c p) t -> p c t", p=128)))

            tok_pj = [None, None]
            pld = {}
            for po in range(2):
                wait("y", st8["wo_free"][po % 2])
                pld[po] = dma("y", lambda e, po=po: e.dma_start(
                    out=wo_sb[:, 1024 * (po % 2):1024 * (po % 2 + 1)],
                    in_=proj_sw.ap()[:, 1024 * po:1024 * (po + 1)]))
            for po in range(DC):
                ppar = po % 2
                if po >= 2:
                    wait("y", st8["wo_free"][ppar])
                    pld[po] = dma("y", lambda e, po=po, ppar=ppar: e.dma_start(
                        out=wo_sb[:, 1024 * ppar:1024 * (ppar + 1)],
                        in_=proj_sw.ap()[:, 1024 * po:1024 * (po + 1)]))
                ty = pld[po]
                wait("t", ty)
                if po == 0:
                    wait("t", tok_amcp)
                    wait("t", tok_ldx)
                bank = pA if ppar == 0 else pB
                if tok_pj[ppar] is not None:
                    wait("t", tok_pj[ppar])
                for c in range(DC):
                    tt = op("t", lambda e, c=c, ppar=ppar, bank=bank: e.matmul(
                        bank[:, :], wo_sb[:, 1024 * ppar + 128 * c:1024 * ppar + 128 * (c + 1)],
                        gn[:, TT * c:TT * (c + 1)],
                        start=(c == 0), stop=(c == DC - 1), skip_group_check=True),
                            1 if c == DC - 1 else 0)
                st8["wo_free"][ppar] = tt
                wait("v", tt)
                tok_pj[ppar] = op("v", lambda e, po=po, bank=bank, xb=xb: e.scalar_tensor_tensor(
                    xb[:, TT * po:TT * (po + 1)], bank[:, :], proj_bc[:, po:po + 1],
                    xb[:, TT * po:TT * (po + 1)], ALU.add, ALU.add), 1)
                if 4 <= po < 4 + NCH and j + 1 < NO:
                    tok_ktm = ktm_round(tok_ldk, tok_amcp if po == 4 else tok_ktm,
                                        po - 4)
            tok_x3 = op("v", lambda e: e.engine_nop(), 1)
            tok_xn3s = ln_pass(xb, y2b, xn, pfree=tok_x3,
                               pre_s=[tok_x3, tok_amcp])[1]
            tok_out = ffn_pass(xb, tok_xn3s)
            wait("g", tok_out)
            _to = dma("g", lambda e, j=j, xb=xb: e.dma_start(
                out=out_t[:, TT * j:TT * (j + 1)].rearrange("(c p) t -> p c t", p=128),
                in_=xb.ap().rearrange("p (c t) -> p c t", c=DC)))
            xt_readers[par] = [_to, tok_out]


    sems = {}
    with ctxm:
        with ExitStack() as semstack:
            names = ["t", "v", "s"] + ["y%d" % i for i in range(NLANE)] + \
                ["g%d" % i for i in range(NLANE)]
            for n in names:
                sems[n] = semstack.enter_context(nc.semaphore("sem_" + n))
            with nc.Block() as block:
                @block.tensor
                def _(eng):
                    schedule(Ctx("t", eng, sems))

                @block.vector
                def _(eng):
                    schedule(Ctx("v", eng, sems))

                @block.scalar
                def _(eng):
                    schedule(Ctx("s", eng, sems))

                @block.sync
                def _(eng):
                    schedule(Ctx("y", eng, sems))

                @block.gpsimd
                def _(eng):
                    schedule(Ctx("g", eng, sems))

    return nc


def stage_inputs(inputs):
    x = np.ascontiguousarray(np.asarray(inputs["L0_attn_x"], np.float32))
    qkv_b = np.asarray(inputs["L0_attn_qkv_b"], np.float32)
    wg_i = np.ascontiguousarray(np.asarray(inputs["L0_ffn_w_gate"], np.int32).T)
    wu_i = np.ascontiguousarray(np.asarray(inputs["L0_ffn_w_up"], np.int32).T)
    wd_i = np.ascontiguousarray(np.asarray(inputs["L0_ffn_w_down"], np.int32).T)
    qkvT = np.ascontiguousarray(np.asarray(inputs["L0_attn_qkv_w"], np.float32).T)
    gateT = np.ascontiguousarray(np.asarray(inputs["L0_attn_gate_w"], np.float32).T)
    projT = np.ascontiguousarray(np.asarray(inputs["L0_attn_proj_w"], np.float32).T)

    def col(v, n):
        return np.ascontiguousarray(np.broadcast_to(v.reshape(n, 128).T, (128, n)))

    qkv_bc = col(qkv_b[0:2048], 16)
    vb_row = np.ascontiguousarray(qkv_b[2048:3072].reshape(1, D))
    gate_bc = col(np.asarray(inputs["L0_attn_gate_b"], np.float32), DC)
    proj_bc = col(np.asarray(inputs["L0_attn_proj_b"], np.float32), DC)
    ln_gc = col(np.asarray(inputs["L0_attn_ln_g"], np.float32), DC)
    ln_bc = col(np.asarray(inputs["L0_attn_ln_b"], np.float32), DC)
    mask_tri = np.triu(np.ones((C, C), np.float32))
    identm = np.eye(C).astype(ml_dtypes.bfloat16)
    masks = {0: (0., 0., 0.), 1: (0., 0., 1.), 2: (1., 1., 0.), 3: (1., 1., 1.)}
    consts = np.ascontiguousarray(
        np.broadcast_to(np.array([EPS, -0.5], np.float32), (128, 2)))
    import os as _os
    nonce = (int.from_bytes(_os.urandom(4), "little") & 0x7FFFFFFF) | 1
    nonce_arr = np.full((1, 16), nonce, np.int32)

    in_maps = []
    OWN = T // 4
    for c in range(8):
        b, r = c // 4, c % 4
        own = x[b, r * OWN:(r + 1) * OWN]
        if r == 2:
            rec = x[b, 0:OWN]
        elif r == 3:
            rec = x[b, OWN:2 * OWN]
        else:
            rec = np.zeros_like(own)
        xt = np.ascontiguousarray(np.concatenate([rec, own], axis=0).T)
        mc = np.ascontiguousarray(
            np.broadcast_to(np.array(masks[r], np.float32), (128, 3)))
        in_maps.append({
            "xt": xt, "wgT_i": wg_i, "wuT_i": wu_i, "wdT_i": wd_i,
            "qkvT": qkvT, "gateT": gateT, "projT": projT,
            "qkv_bc": qkv_bc, "vb_row": vb_row, "gate_bc": gate_bc,
            "proj_bc": proj_bc, "ln_gc": ln_gc, "ln_bc": ln_bc,
            "mcols": mc, "mask_tri": mask_tri, "ident": identm,
            "consts_i": consts, "nonce_i": nonce_arr,
        })
    return in_maps


def kernel(**inputs):
    in_maps = stage_inputs(inputs)
    nc = build_program()
    res = run_bass_kernel_spmd(nc, in_maps, core_ids=list(range(8)))
    out = np.zeros((B, T, D), np.float32)
    OWN = T // 4
    for c in range(8):
        b, r = c // 4, c % 4
        out[b, r * OWN:(r + 1) * OWN] = res.results[c]["out_t"].T
    return out


# version 11 (final)
# speedup vs baseline: 1.2878x; 1.1338x over previous
import numpy as np
import ml_dtypes
import concourse.bass as bass
import concourse.mybir as mybir
from concourse.bass_utils import run_bass_kernel_spmd
from contextlib import ExitStack

F32 = mybir.dt.float32
BF16 = mybir.dt.bfloat16
I32 = mybir.dt.int32
ALU = mybir.AluOpType
AF = mybir.ActivationFunctionType
EPS = 1e-5

B, T, D = 2, 16384, 1024
H, DH, C = 16, 64, 128
HID = 3072
DC = D // 128
HO = HID // 128
TT = 512
NCH = TT // C
NF = 16
NO = 8
NP = 8
AUG = DH + 1
PW = 2 * AUG
SQW = NP * PW

DEBUG = False
RACE_DETECT = True
SIM_LOCAL_SYNC = False


NLANE = 4


class Ctx:

    def __init__(self, live, eng, sems):
        self.live = live
        self.eng = eng
        self.sems = sems
        self.c = {k: 0 for k in sems}
        self.lane_i = {"y": 0, "g": 0}

    def op(self, eng, fn, inc=0, sem=None, chain=True):
        if sem is None:
            sem = eng
        chain = chain and eng in ("v", "s") and sem == eng
        if chain:
            if self.live == eng and self.c[eng] > 0:
                self.eng.wait_ge(self.sems[eng], self.c[eng])
            inc = 1
        if self.live == eng:
            inst = fn(self.eng)
            if inc:
                inst.then_inc(self.sems[sem], inc)
        if inc:
            self.c[sem] += inc
        return (sem, self.c[sem])

    def dma(self, eng, fn):
        lane = "%s%d" % (eng, self.lane_i[eng] % NLANE)
        self.lane_i[eng] += 1
        self.wait(eng, (lane, self.c[lane]))
        return self.op(eng, fn, 16, sem=lane)

    def wait(self, eng, tok):
        if tok is None:
            return
        sem, val = tok
        if val <= 0:
            return
        if self.live == eng:
            self.eng.wait_ge(self.sems[sem], val)

    def tok(self, sem):
        return (sem, self.c[sem])


def build_program():
    nc = bass.Bass("TRN2", target_bir_lowering=False,
                   detect_race_conditions=RACE_DETECT)

    xt = nc.dram_tensor("xt", [D, NF * TT], F32, kind="ExternalInput").ap()
    wgT_i = nc.dram_tensor("wgT_i", [D, HID], I32, kind="ExternalInput").ap()
    wuT_i = nc.dram_tensor("wuT_i", [D, HID], I32, kind="ExternalInput").ap()
    wdT_i = nc.dram_tensor("wdT_i", [HID, D], I32, kind="ExternalInput").ap()
    qkvT_i = nc.dram_tensor("qkvT", [D, 3 * D], F32, kind="ExternalInput").ap()
    gateT_i = nc.dram_tensor("gateT", [D, D], F32, kind="ExternalInput").ap()
    projT_i = nc.dram_tensor("projT", [D, D], F32, kind="ExternalInput").ap()
    qkv_bc_i = nc.dram_tensor("qkv_bc", [128, 16], F32, kind="ExternalInput").ap()
    vb_row_i = nc.dram_tensor("vb_row", [1, D], F32, kind="ExternalInput").ap()
    gate_bc_i = nc.dram_tensor("gate_bc", [128, DC], F32, kind="ExternalInput").ap()
    proj_bc_i = nc.dram_tensor("proj_bc", [128, DC], F32, kind="ExternalInput").ap()
    ln_gc_i = nc.dram_tensor("ln_gc", [128, DC], F32, kind="ExternalInput").ap()
    ln_bc_i = nc.dram_tensor("ln_bc", [128, DC], F32, kind="ExternalInput").ap()
    mcols_i = nc.dram_tensor("mcols", [128, 3], F32, kind="ExternalInput").ap()
    mask_i = nc.dram_tensor("mask_tri", [C, C], F32, kind="ExternalInput").ap()
    consts_i = nc.dram_tensor("consts_i", [128, 2], F32, kind="ExternalInput").ap()
    nonce_i = nc.dram_tensor("nonce_i", [1, 16], I32, kind="ExternalInput").ap()
    ident_i = nc.dram_tensor("ident", [C, C], BF16, kind="ExternalInput").ap()
    out_t = nc.dram_tensor("out_t", [D, NO * TT], F32, kind="ExternalOutput").ap()

    wg_sw = nc.dram_tensor("wg_sw", [128, HO * 1024], BF16)
    wu_sw = nc.dram_tensor("wu_sw", [128, HO * 1024], BF16)
    wd_sw = nc.dram_tensor("wd_sw", [128, DC * HID], BF16)
    qk_sw = nc.dram_tensor("qk_sw", [128, 16 * 1024], BF16)
    v_sw = nc.dram_tensor("v_sw", [128, DC * D], BF16)
    gate_sw = nc.dram_tensor("gate_sw", [128, DC * 1024], BF16)
    proj_sw = nc.dram_tensor("proj_sw", [128, DC * 1024], BF16)
    bx2 = nc.dram_tensor("bx2", [D, NO * TT], F32)
    bq = nc.dram_tensor("bq", [D, NO * TT], BF16)
    bk = nc.dram_tensor("bk", [D, NO * TT], BF16)
    bva = nc.dram_tensor("bva", [128, NO * NCH * SQW], BF16)
    shd = nc.dram_tensor("shd", [512, SQW], F32, addr_space="Shared")
    shf = nc.dram_tensor("shf", [2, 16], I32, addr_space="Shared")

    ctxm = ExitStack()
    sbuf = lambda name, shape, dt: ctxm.enter_context(nc.sbuf_tensor(name, shape, dt))
    psum = lambda name, shape, dt: ctxm.enter_context(nc.psum_tensor(name, shape, dt))

    x_tA = sbuf("x_tA", [128, DC * TT], F32)
    xn = sbuf("xn", [128, DC * TT], BF16)
    tmp = sbuf("tmp", [128, DC * TT], BF16)
    y2b = sbuf("y2b", [128, DC * TT], BF16)
    gn = sbuf("gn", [128, DC * TT], BF16)
    Q = sbuf("Q", [128, DC * TT], BF16)
    K = sbuf("K", [128, DC * TT], BF16)
    v_aug = sbuf("v_aug", [128, NCH * SQW], BF16)
    h_sb = sbuf("h_sb", [128, HO * TT], BF16)
    k_tm = sbuf("k_tm", [128, NCH * NP * C], BF16)
    s_Am = sbuf("s_Am", [128, 2 * C], BF16)
    S_sb0 = sbuf("S_sb0", [128, H * AUG], BF16)
    S_sb1 = sbuf("S_sb1", [128, H * AUG], BF16)
    S_sbs = [S_sb0, S_sb1]
    S_start = sbuf("S_start", [128, SQW], F32)
    s_rec = sbuf("s_rec", [128, SQW], F32)
    s_own = sbuf("s_own", [128, SQW], F32)
    E2p = sbuf("E2p", [128, SQW], F32)
    E1p = s_own
    rows_f = sbuf("rows_f", [1, 5 * TT], F32)
    rows_b = sbuf("rows_b", [1, 4 * TT], BF16)
    wbuf_g = sbuf("wbuf_g", [128, 3 * 1024], BF16)
    wbuf_u = sbuf("wbuf_u", [128, 3 * 1024], BF16)
    wd_sb = sbuf("wd_sb", [128, 2 * HID], BF16)
    wv_sb = sbuf("wv_sb", [128, DC * D], BF16)
    wo_sb = sbuf("wo_sb", [128, 2 * 1024], BF16)
    wint = sbuf("wint", [128, HID], I32)
    wcast = sbuf("wcast", [128, HID], BF16)
    _wint_off = nc.lookup_mloc(wint).addr
    x_tB = nc.alloc_sbuf_tensor_at("x_tB", [128, DC * TT], F32, offset=_wint_off)
    x_ts = [x_tA, x_tB]
    qkv_bc = sbuf("qkv_bcs", [128, 16], F32)
    gate_bc = sbuf("gate_bcs", [128, DC], F32)
    proj_bc = sbuf("proj_bcs", [128, DC], F32)
    ln_gc = sbuf("ln_gcs", [128, DC], F32)
    ln_bc = sbuf("ln_bcs", [128, DC], F32)
    mcols = sbuf("mcolss", [128, 3], F32)
    vbias = sbuf("vbias", [128, D], F32)
    mask_tri = sbuf("mask_tris", [C, C], F32)
    ident = sbuf("idents", [C, C], BF16)
    onescol = sbuf("onescol", [128, 1], BF16)
    negcol = sbuf("negcol", [128, 1], BF16)
    ones1r = sbuf("ones1r", [1, 128], BF16)
    consts_sb = sbuf("consts_sb", [128, 2], F32)
    nc.const_aps.register(F32, EPS, consts_sb[:, 0:1])
    nc.const_aps.register(F32, -0.5, consts_sb[:, 1:2])
    den_sb = sbuf("den_sb", [128, 16], F32)
    rd_sb = sbuf("rd_sb", [128, 16], F32)

    pA = psum("pA", [128, 512], F32)
    pB = psum("pB", [128, 512], F32)
    pC = psum("pC", [128, 512], F32)
    pD = psum("pD", [128, 512], F32)
    pT = psum("pT", [128, 1024], BF16)
    pS0 = psum("pS0", [128, 3 * PW], F32)
    pS1 = psum("pS1", [128, 3 * PW], F32)
    pS2 = psum("pS2", [128, 512], F32)
    pS = [pS0, pS1, pS2]

    def pS_blk(p):
        return pS[p // 3][:, (p % 3) * PW:(p % 3) * PW + PW]

    dbg = {}
    if DEBUG:
        dbg["d_x2"] = nc.dram_tensor("d_x2", [128, DC * TT], F32, kind="ExternalOutput").ap()
        for nme in ["d_xn2", "d_q", "d_k", "d_gn"]:
            dbg[nme] = nc.dram_tensor(nme, [128, DC * TT], BF16, kind="ExternalOutput").ap()
        dbg["d_va"] = nc.dram_tensor("d_va", [128, NCH * SQW], BF16, kind="ExternalOutput").ap()
        dbg["d_sstart"] = nc.dram_tensor("d_sstart", [128, SQW], F32, kind="ExternalOutput").ap()
        dbg["d_otm"] = nc.dram_tensor("d_otm", [128, 1024], BF16, kind="ExternalOutput").ap()

    def schedule(cx: Ctx):
        op, wait, tok = cx.op, cx.wait, cx.tok
        dma = cx.dma
        st8 = {"upd_first": True, "SrecEvac": None, "wv_free": None,
               "bq": None, "bk": None, "bva": None,
               "wgu_free": [None, None], "wd_free": [None, None],
               "wo_free": [None, None]}

        if cx.live == "g":
            e = cx.eng
            cid = e.partition_id()
            par = cid % 2
            opar = (cid + 1) % 2
            my_flag = shf.ap()[bass.ts(par, 1), 0:1]
            pa_flag = shf.ap()[bass.ts(opar, 1), 0:1]
            st8["g"] = (par, opar, my_flag, pa_flag)

        _ct = [
            dma("y", lambda e: e.dma_start(out=mask_tri[:, :], in_=mask_i)),
            dma("y", lambda e: e.dma_start(out=ident[:, :], in_=ident_i)),
            dma("y", lambda e: e.dma_start(out=qkv_bc[:, :], in_=qkv_bc_i)),
            dma("y", lambda e: e.dma_start(out=gate_bc[:, :], in_=gate_bc_i)),
            dma("y", lambda e: e.dma_start(out=proj_bc[:, :], in_=proj_bc_i)),
            dma("y", lambda e: e.dma_start(out=ln_gc[:, :], in_=ln_gc_i)),
            dma("y", lambda e: e.dma_start(out=ln_bc[:, :], in_=ln_bc_i)),
            dma("y", lambda e: e.dma_start(out=mcols[:, :], in_=mcols_i)),
            dma("y", lambda e: e.dma_start(out=consts_sb[:, :], in_=consts_i)),
        ]
        tok_consts = _ct[-NLANE:]
        op("v", lambda e: e.memset(onescol[:, :], 1.0))
        op("v", lambda e: e.memset(negcol[:, :], -1.0))
        op("v", lambda e: e.memset(ones1r[:, :], 1.0))
        op("v", lambda e: e.memset(v_aug[:, :], 1.0))
        op("v", lambda e: e.memset(S_sb0[:, :], 0.0), 1)
        tok_init_v = op("v", lambda e: e.memset(S_sb1[:, :], 0.0), 1)

        tv = None
        tg = None
        for wi, wsw, scl, cols, pat in (
                (wgT_i, wg_sw, 1.0 / 127.0, HID, "g"),
                (wuT_i, wu_sw, 1.0 / 127.0, HID, "g"),
                (wdT_i, wd_sw, 1.0 / 254.0, D, "d")):
            nchunks = DC if cols == HID else HO
            for c in range(nchunks):
                if tv is not None:
                    wait("y", tv)
                tk = dma("y", lambda e, wi=wi, c=c, cols=cols: e.dma_start(
                    out=wint[:, 0:cols], in_=wi[128 * c:128 * (c + 1), :]))
                wait("v", tk)
                if tg is not None:
                    wait("v", tg)
                tv = op("v", lambda e, scl=scl, cols=cols: e.tensor_scalar_mul(
                    wcast[:, 0:cols], wint[:, 0:cols], scl), 1)
                wait("g", tv)
                if pat == "g":
                    tg = dma("g", lambda e, wsw=wsw, c=c: e.dma_start(
                        out=wsw.ap().rearrange("p (ho c i) -> p ho c i",
                                               ho=HO, c=DC)[:, :, c, :],
                        in_=wcast.ap().rearrange("p (ho i) -> p ho i", ho=HO)))
                else:
                    tg = dma("g", lambda e, c=c: e.dma_start(
                        out=wd_sw.ap().rearrange("p (do hc i) -> p do hc i",
                                                 do=DC, hc=HO)[:, :, c, :],
                        in_=wcast[:, 0:D].rearrange("p (do i) -> p do i", do=DC)))
        lasty = []
        for c in range(DC):
            wait("g", tv)
            wait("g", tg)
            for _t in lasty:
                wait("g", _t)
            tg = dma("g", lambda e, c=c: e.dma_start(
                out=wcast[:, :], in_=qkvT_i[128 * c:128 * (c + 1), :]))
            wait("y", tg)
            _q1 = dma("y", lambda e, c=c: e.dma_start(
                out=qk_sw.ap().rearrange("p (o c i) -> p o c i", o=16, c=DC)[:, :, c, :],
                in_=wcast[:, 0:2048].rearrange("p (o i) -> p o i", o=16)))
            _q2 = dma("y", lambda e, c=c: e.dma_start(
                out=v_sw.ap()[:, D * c:D * (c + 1)], in_=wcast[:, 2048:3072]))
            lasty = [_q1, _q2]
        for src, dst in ((gateT_i, gate_sw), (projT_i, proj_sw)):
            for c in range(DC):
                for _t in lasty:
                    wait("g", _t)
                tg = dma("g", lambda e, src=src, c=c: e.dma_start(
                    out=wcast[:, 0:D], in_=src[128 * c:128 * (c + 1), :]))
                wait("y", tg)
                lasty = [dma("y", lambda e, dst=dst, c=c: e.dma_start(
                    out=dst.ap().rearrange("p (o c i) -> p o c i", o=DC, c=DC)[:, :, c, :],
                    in_=wcast[:, 0:D].rearrange("p (o i) -> p o i", o=DC)))]
        for _t in lasty:
            wait("g", _t)
        tg = dma("g", lambda e: e.dma_start(out=wcast[0:1, 0:D], in_=vb_row_i))
        wait("t", tg)
        wait("t", tok_init_v)
        tvb = None
        for half in range(2):
            if tvb is not None:
                wait("t", tvb)
            tth = op("t", lambda e, half=half: e.matmul(
                pA[:, :], ones1r[:, :], wcast[0:1, 512 * half:512 * (half + 1)],
                start=True, stop=True, skip_group_check=True), 1)
            wait("v", tth)
            tvb = op("v", lambda e, half=half: e.tensor_copy(
                vbias[:, 512 * half:512 * (half + 1)], pA[:, :]), 1)
        tok_vbias = tvb
        dequant_done = [tok_vbias, tok("t")] + list(lasty)

        st8["wv_free"] = tok("t")
        st8["wgu_free"] = [tok("t"), tok("t"), tok("t")]
        st8["wd_free"] = [tok("t"), tok("t")]
        st8["wo_free"] = [tok("t"), tok("t")]

        def ln_pass(xb, sq_scratch, out_buf, extra=None, pfree=None,
                    pre_s=(), out_ready=None, defer=False):
            R = lambda i: rows_f[0:1, TT * i:TT * (i + 1)]
            RB = lambda i: rows_b[0:1, TT * i:TT * (i + 1)]
            for tk in pre_s:
                wait("s", tk)
            ts1 = op("s", lambda e: e.activation(tmp[:, :], xb[:, :], AF.Copy), 1)
            ts2 = op("s", lambda e, b=sq_scratch: e.activation(
                b[:, :], tmp[:, :], AF.Square), 1)
            wait("t", ts1)
            wait("t", pfree)
            for c in range(DC):
                op("t", lambda e, c=c: e.matmul(
                    pC[0:1, :], negcol[:, :], tmp[:, TT * c:TT * (c + 1)],
                    start=(c == 0), stop=(c == DC - 1), skip_group_check=True))
            wait("t", ts2)
            for c in range(DC):
                tt = op("t", lambda e, c=c, b=sq_scratch: e.matmul(
                    pD[0:1, :], onescol[:, :], b[:, TT * c:TT * (c + 1)],
                    start=(c == 0), stop=(c == DC - 1), skip_group_check=True),
                        1 if c == DC - 1 else 0)
            wait("v", tt)
            tvr = op("v", lambda e: e.tensor_scalar_mul(RB(0), pC[0:1, :], 1.0 / D), 1)
            op("v", lambda e: e.tensor_mul(R(2), RB(0), RB(0)))
            tvv2 = op("v", lambda e: e.scalar_tensor_tensor(
                R(3), pD[0:1, :], 1.0 / D, R(2), ALU.mult, ALU.subtract), 1)
            wait("s", tvv2)
            op("s", lambda e: e.activation(R(4), R(3), AF.Ln, bias=EPS))
            ts3 = op("s", lambda e: e.activation(RB(1), R(4), AF.Exp, scale=-0.5), 1)
            if extra == "ln2":
                op("v", lambda e: e.tensor_scalar_add(R(1), R(3), EPS))
                tv2 = op("v", lambda e: e.reciprocal(R(2), R(1)), 1)
                tv2b = op("v", lambda e: e.tensor_mul(R(1), R(3), R(2)), 1)
                wait("s", tv2b)
                op("s", lambda e: e.activation(R(2), R(1), AF.Ln, bias=EPS))
                ts4 = op("s", lambda e: e.activation(R(1), R(2), AF.Exp, scale=-0.5), 1)
                wait("v", ts4)
                tva = op("v", lambda e: e.tensor_mul(RB(2), RB(1), R(1)), 1)
            def ln_finish():
                wait("t", tvv2)
                if extra == "ln2":
                    wait("t", tv2b)
                tt2 = op("t", lambda e: e.matmul(pD[:, :], ones1r[:, :], RB(0),
                                                 start=True, stop=True,
                                                 skip_group_check=True))
                wait("t", ts3)
                tt3 = op("t", lambda e: e.matmul(pC[:, :], ones1r[:, :], RB(1),
                                                 start=True, stop=True,
                                                 skip_group_check=True),
                         0 if extra == "ln2" else 1)
                if extra == "ln2":
                    wait("t", tva)
                    tt3 = op("t", lambda e: e.matmul(pB[:, :], ones1r[:, :], RB(2),
                                                     start=True, stop=True,
                                                     skip_group_check=True), 1)
                wait("v", tt3)
                if out_ready is not None:
                    wait("v", out_ready)
                chunk_toks = []
                y2_toks = []
                if extra == "ln2":
                    for c in range(DC):
                        op("v", lambda e, c=c: e.tensor_add(
                            tmp[:, TT * c:TT * (c + 1)], xb[:, TT * c:TT * (c + 1)],
                            pD[:, :]))
                        y2_toks.append(op("v", lambda e, c=c: e.tensor_mul(
                            y2b[:, TT * c:TT * (c + 1)], tmp[:, TT * c:TT * (c + 1)],
                            pC[:, :]), 1))
                    ug = None
                    for c in range(DC):
                        op("v", lambda e, c=c: e.tensor_mul(
                            gn[:, TT * c:TT * (c + 1)], tmp[:, TT * c:TT * (c + 1)],
                            pB[:, :]))
                        if c == DC - 1:
                            ug = op("v", lambda e: e.engine_nop(), 1)
                        chunk_toks.append(op("v", lambda e, c=c, ob=out_buf: e.tensor_scalar(
                            ob[:, TT * c:TT * (c + 1)], gn[:, TT * c:TT * (c + 1)],
                            ln_gc[:, c:c + 1], ln_bc[:, c:c + 1], ALU.mult, ALU.add), 1))
                    return chunk_toks[-1], chunk_toks, y2_toks, ug
                else:
                    for c in range(DC):
                        op("v", lambda e, c=c: e.tensor_add(
                            tmp[:, TT * c:TT * (c + 1)], xb[:, TT * c:TT * (c + 1)],
                            pD[:, :]))
                        chunk_toks.append(op("v", lambda e, c=c, ob=out_buf: e.tensor_mul(
                            ob[:, TT * c:TT * (c + 1)], tmp[:, TT * c:TT * (c + 1)],
                            pC[:, :]), 1))
                    return chunk_toks[-1], chunk_toks, None, None
            if defer:
                return ln_finish
            return ln_finish()

        def ffn_pass(xb, xn_toks):
            tok_h = []
            for ho in range(HO):
                par = ho % 3
                wait("y", st8["wgu_free"][par])
                tyg = dma("y", lambda e, ho=ho, par=par: e.dma_start(
                    out=wbuf_g[:, 1024 * par:1024 * (par + 1)],
                    in_=wg_sw.ap()[:, 1024 * ho:1024 * (ho + 1)]))
                tyu = dma("y", lambda e, ho=ho, par=par: e.dma_start(
                    out=wbuf_u[:, 1024 * par:1024 * (par + 1)],
                    in_=wu_sw.ap()[:, 1024 * ho:1024 * (ho + 1)]))
                wait("t", tyg)
                wait("t", tyu)
                gbank = pA if ho % 2 == 0 else pC
                ubank = pB if ho % 2 == 0 else pD
                if ho >= 2:
                    wait("t", tok_h[ho - 2][0])
                for c in range(DC):
                    if ho == 0:
                        wait("t", xn_toks[c])
                    tt_g = op("t", lambda e, c=c, par=par, gbank=gbank: e.matmul(
                        gbank[:, :], wbuf_g[:, 1024 * par + 128 * c:1024 * par + 128 * (c + 1)],
                        xn[:, TT * c:TT * (c + 1)],
                        start=(c == 0), stop=(c == DC - 1), skip_group_check=True),
                            1 if c == DC - 1 else 0)
                if ho >= 2:
                    wait("t", tok_h[ho - 2][1])
                for c in range(DC):
                    tt_u = op("t", lambda e, c=c, par=par, ubank=ubank: e.matmul(
                        ubank[:, :], wbuf_u[:, 1024 * par + 128 * c:1024 * par + 128 * (c + 1)],
                        xn[:, TT * c:TT * (c + 1)],
                        start=(c == 0), stop=(c == DC - 1), skip_group_check=True),
                            1 if c == DC - 1 else 0)
                st8["wgu_free"][par] = tt_u
                wait("s", tt_g)
                if ho >= 2:
                    wait("s", tok_h[ho - 2][1])
                ts = op("s", lambda e, ho=ho, gbank=gbank: e.activation(
                    tmp[:, 512 * (ho % 2):512 * (ho % 2 + 1)], gbank[:, :], AF.Silu), 1)
                wait("v", ts)
                wait("v", tt_u)
                tv = op("v", lambda e, ho=ho, ubank=ubank: e.tensor_mul(
                    h_sb[:, TT * ho:TT * (ho + 1)],
                    tmp[:, 512 * (ho % 2):512 * (ho % 2 + 1)], ubank[:, :]), 1)
                tok_h.append((ts, tv))
            tok_hall = tok_h[HO - 1][1]
            tok_dn = [None, None]
            for do in range(DC):
                par = do % 2
                wait("y", st8["wd_free"][par])
                ty = dma("y", lambda e, do=do, par=par: e.dma_start(
                    out=wd_sb[:, HID * par:HID * (par + 1)],
                    in_=wd_sw.ap()[:, HID * do:HID * (do + 1)]))
                wait("t", ty)
                wait("t", tok_hall)
                if do == 0:
                    wait("t", tok_h[HO - 1][0])
                if do == 1:
                    wait("t", tok_h[HO - 2][1])
                if tok_dn[par] is not None:
                    wait("t", tok_dn[par])
                bank = pC if par == 0 else pD
                for hc in range(HO):
                    tt = op("t", lambda e, hc=hc, par=par, bank=bank: e.matmul(
                        bank[:, :], wd_sb[:, HID * par + 128 * hc:HID * par + 128 * (hc + 1)],
                        h_sb[:, TT * hc:TT * (hc + 1)],
                        start=(hc == 0), stop=(hc == HO - 1), skip_group_check=True),
                            1 if hc == HO - 1 else 0)
                st8["wd_free"][par] = tt
                wait("v", tt)
                tok_dn[par] = op("v", lambda e, do=do, bank=bank, xb=xb: e.tensor_add(
                    xb[:, TT * do:TT * (do + 1)], xb[:, TT * do:TT * (do + 1)],
                    bank[:, :]), 1)
            return op("v", lambda e: e.engine_nop(), 1)

        xt_readers = {0: [tok_vbias], 1: list(dequant_done)}
        tok_upd_t = tok("t")
        pend_xn1 = None
        bounce_qkv = []
        for sl in range(NF):
            own = sl >= NO
            j = sl - NO
            par = sl % 2
            xb = x_ts[par]
            if sl == 0:
                for tk in xt_readers[0]:
                    wait("y", tk)
                xt_readers[0] = []
                ty0 = dma("y", lambda e: e.dma_start(
                    out=x_tA.ap().rearrange("p (c t) -> p c t", c=DC),
                    in_=xt[:, 0:TT].rearrange("(c p) t -> p c t", p=128)))
                for _t in tok_consts:
                    wait("s", _t)
                wait("s", tok_vbias)
                pend_xn1 = ln_pass(xb, y2b, xn, pfree=tok_vbias, pre_s=[ty0])[1]
            tok_x2 = ffn_pass(xb, pend_xn1)
            bt_x2 = None
            if own:
                wait("g", tok_x2)
                bt_x2 = dma("g", lambda e, j=j, xb=xb: e.dma_start(
                    out=bx2.ap()[:, TT * j:TT * (j + 1)].rearrange("(c p) t -> p c t", p=128),
                    in_=xb.ap().rearrange("p (c t) -> p c t", c=DC)))
            tok_xn2, _, y2_toks, tok_ug = ln_pass(xb, gn, xn, extra="ln2",
                                                  pfree=tok_x2, pre_s=[tok_x2])
            gs_toks = []
            for go in range(DC):
                gpar = go % 2
                wait("y", st8["wo_free"][gpar])
                ty = dma("y", lambda e, go=go, gpar=gpar: e.dma_start(
                    out=wo_sb[:, 1024 * gpar:1024 * (gpar + 1)],
                    in_=gate_sw.ap()[:, 1024 * go:1024 * (go + 1)]))
                wait("t", ty)
                if go == 1:
                    wait("t", tok_ug)
                bank = pA if gpar == 0 else pB
                if go >= 2:
                    wait("t", gs_toks[go - 2])
                for c in range(DC):
                    if go == 0:
                        wait("t", y2_toks[c])
                    tt = op("t", lambda e, c=c, gpar=gpar, bank=bank: e.matmul(
                        bank[:, :], wo_sb[:, 1024 * gpar + 128 * c:1024 * gpar + 128 * (c + 1)],
                        y2b[:, TT * c:TT * (c + 1)],
                        start=(c == 0), stop=(c == DC - 1), skip_group_check=True),
                            1 if c == DC - 1 else 0)
                st8["wo_free"][gpar] = tt
                wait("s", tt)
                if go == 0:
                    wait("s", tok_xn2)
                ts2 = op("s", lambda e, go=go, bank=bank: e.activation(
                    tmp[:, TT * go:TT * (go + 1)], bank[:, :], AF.Sigmoid,
                    bias=gate_bc[:, go:go + 1]), 1)
                gs_toks.append(ts2)
            wait("t", gs_toks[-1])
            wait("t", tok_xn2)
            for c in range(DC):
                tt = op("t", lambda e, c=c: e.matmul(
                    pC[0:1, :], onescol[:, :], tmp[:, TT * c:TT * (c + 1)],
                    start=(c == 0), stop=(c == DC - 1), skip_group_check=True),
                        1 if c == DC - 1 else 0)
            wait("v", tt)
            op("v", lambda e: e.tensor_scalar(rows_f[0:1, 0:TT], pC[0:1, :],
                                              1.0 / D, EPS, ALU.mult, ALU.add))
            tv = op("v", lambda e: e.reciprocal(rows_f[0:1, TT:2 * TT],
                                                rows_f[0:1, 0:TT]), 1)
            wait("v", tv)
            tv = op("v", lambda e: e.tensor_copy(rows_b[0:1, 3 * TT:4 * TT],
                                                 rows_f[0:1, TT:2 * TT]), 1)
            wait("t", tv)
            tt = op("t", lambda e: e.matmul(pD[:, :], ones1r[:, :],
                                            rows_b[0:1, 3 * TT:4 * TT],
                                            start=True, stop=True,
                                            skip_group_check=True), 1)
            wait("v", tt)
            for c in range(DC):
                op("v", lambda e, c=c: e.tensor_mul(
                    gn[:, TT * c:TT * (c + 1)], tmp[:, TT * c:TT * (c + 1)], pD[:, :]))
            tok_gn = op("v", lambda e: e.engine_nop(), 1)

            def qkpart(offs, dest, prev4):
                toks = []
                banks4 = [pA, pB, pC, pD]
                for i in range(DC):
                    o = offs + i
                    qpar = o % 2
                    wait("y", st8["wo_free"][qpar])
                    ty = dma("y", lambda e, o=o, qpar=qpar: e.dma_start(
                        out=wo_sb[:, 1024 * qpar:1024 * (qpar + 1)],
                        in_=qk_sw.ap()[:, 1024 * o:1024 * (o + 1)]))
                    wait("t", ty)
                    bank = banks4[i % 4]
                    if i < 4:
                        if prev4[i] is not None:
                            wait("t", prev4[i])
                    else:
                        wait("t", toks[i - 4])
                    for c in range(DC):
                        tt = op("t", lambda e, c=c, qpar=qpar, bank=bank: e.matmul(
                            bank[:, :],
                            wo_sb[:, 1024 * qpar + 128 * c:1024 * qpar + 128 * (c + 1)],
                            xn[:, TT * c:TT * (c + 1)],
                            start=(c == 0), stop=(c == DC - 1), skip_group_check=True),
                                1 if c == DC - 1 else 0)
                    st8["wo_free"][qpar] = tt
                    wait("v", tt)
                    if i == 0:
                        wait("v", tok_gn)
                        wait("v", st8["bq"] if offs == 0 else st8["bk"])
                    tv = op("v", lambda e, i=i, o=o, bank=bank, d=dest: e.scalar_tensor_tensor(
                        d[:, TT * i:TT * (i + 1)], bank[:, :], qkv_bc[:, o:o + 1],
                        gn[:, TT * i:TT * (i + 1)], ALU.add, ALU.mult), 1)
                    toks.append(tv)
                return toks

            def elu_gate(scr_buf, dest, toks_g):
                out_toks = []
                for c in range(DC):
                    wait("s", toks_g[c])
                    ts = op("s", lambda e, c=c, sb2=scr_buf, d=dest: e.activation(
                        sb2[:, TT * c:TT * (c + 1)], d[:, TT * c:TT * (c + 1)], AF.Exp), 1)
                    op("v", lambda e, c=c, d=dest: e.tensor_scalar_max(
                        h_sb[:, TT * c:TT * (c + 1)], d[:, TT * c:TT * (c + 1)], 0.0))
                    wait("v", ts)
                    tv2 = op("v", lambda e, c=c, sb2=scr_buf, d=dest: e.scalar_tensor_tensor(
                        d[:, TT * c:TT * (c + 1)], sb2[:, TT * c:TT * (c + 1)], 1.0,
                        h_sb[:, TT * c:TT * (c + 1)], ALU.min, ALU.add), 1)
                    out_toks.append(tv2)
                return out_toks

            bounce_qkv = []
            prev4_0 = [tok_gn, None, tok_gn, tok_gn]
            if own:
                q_raw = qkpart(0, Q, prev4_0)
                q_toks = elu_gate(tmp, Q, q_raw)
                wait("g", q_toks[-1])
                st8["bq"] = dma("g", lambda e, j=j: e.dma_start(
                    out=bq.ap()[:, TT * j:TT * (j + 1)].rearrange("(c p) t -> p c t", p=128),
                    in_=Q.ap().rearrange("p (c t) -> p c t", c=DC)))
                bounce_qkv.append(st8["bq"])
                prev4_k = [q_raw[4], q_raw[5], q_raw[6], q_raw[7]]
            else:
                q_toks = None
                prev4_k = prev4_0
            k_raw = qkpart(8, K, prev4_k)
            k_toks = elu_gate(y2b, K, k_raw)
            tok_k = k_toks[-1]
            if own:
                wait("g", tok_k)
                st8["bk"] = dma("g", lambda e, j=j: e.dma_start(
                    out=bk.ap()[:, TT * j:TT * (j + 1)].rearrange("(c p) t -> p c t", p=128),
                    in_=K.ap().rearrange("p (c t) -> p c t", c=DC)))
                bounce_qkv.append(st8["bk"])

            wait("y", st8["wv_free"])
            tyv = dma("y", lambda e: e.dma_start(out=wv_sb[:, :], in_=v_sw.ap()))
            tok_va = []
            for tc in range(NCH):
                b0 = pA if tc % 2 == 0 else pC
                b1 = pB if tc % 2 == 0 else pD
                wait("t", tyv)
                if tc == 0:
                    wait("t", tok_xn2)
                    wait("t", k_raw[4])
                    wait("t", k_raw[5])
                if tc == 1:
                    wait("t", k_raw[6])
                    wait("t", k_raw[7])
                if tc >= 2:
                    wait("t", tok_va[tc - 2])
                for c in range(DC):
                    op("t", lambda e, c=c, tc=tc, b0=b0: e.matmul(
                        b0[:, :], xn[:, TT * c + C * tc:TT * c + C * (tc + 1)],
                        wv_sb[:, D * c:D * c + 512],
                        start=(c == 0), stop=(c == DC - 1), skip_group_check=True))
                    tt = op("t", lambda e, c=c, tc=tc, b1=b1: e.matmul(
                        b1[:, :], xn[:, TT * c + C * tc:TT * c + C * (tc + 1)],
                        wv_sb[:, D * c + 512:D * (c + 1)],
                        start=(c == 0), stop=(c == DC - 1), skip_group_check=True),
                            1 if c == DC - 1 else 0)
                wait("v", tt)
                if tc == 0:
                    wait("v", st8["bva"])
                va4 = v_aug.ap().rearrange("p (tc h a) -> p tc h a", tc=NCH, h=H)
                op("v", lambda e, tc=tc, b0=b0, va4=va4: e.tensor_add(
                    va4[:, tc:tc + 1, 0:NP, 0:DH],
                    b0.ap().rearrange("p (o h i) -> p o h i", o=1, h=NP),
                    vbias[:, 0:512].rearrange("p (o h i) -> p o h i", o=1, h=NP)))
                tvv = op("v", lambda e, tc=tc, b1=b1, va4=va4: e.tensor_add(
                    va4[:, tc:tc + 1, NP:H, 0:DH],
                    b1.ap().rearrange("p (o h i) -> p o h i", o=1, h=NP),
                    vbias[:, 512:1024].rearrange("p (o h i) -> p o h i", o=1, h=NP)), 1)
                tok_va.append(tvv)
            st8["wv_free"] = tok("t")
            if own:
                wait("g", tok_va[-1])
                st8["bva"] = dma("g", lambda e, j=j: e.dma_start(
                    out=bva.ap()[:, NCH * SQW * j:NCH * SQW * (j + 1)], in_=v_aug[:, :]))
                bounce_qkv.append(st8["bva"])
                if DEBUG and sl == NO:
                    bounce_qkv.append(dma("g", lambda e, xb=xb: e.dma_start(
                        out=dbg["d_x2"], in_=xb[:, :])))
                    bounce_qkv.append(dma("g", lambda e: e.dma_start(out=dbg["d_xn2"], in_=xn[:, :])))
                    bounce_qkv.append(dma("g", lambda e: e.dma_start(out=dbg["d_q"], in_=Q[:, :])))
                    bounce_qkv.append(dma("g", lambda e: e.dma_start(out=dbg["d_k"], in_=K[:, :])))
                    bounce_qkv.append(dma("g", lambda e: e.dma_start(out=dbg["d_gn"], in_=gn[:, :])))
                    bounce_qkv.append(dma("g", lambda e: e.dma_start(out=dbg["d_va"], in_=v_aug[:, :])))

            ln_fin = None
            if sl + 1 < NF:
                npar = (sl + 1) % 2
                for tk in xt_readers[npar]:
                    wait("g", tk)
                xt_readers[npar] = []
                tyn = dma("g", lambda e, sl=sl, npar=npar: e.dma_start(
                    out=x_ts[npar].ap().rearrange("p (c t) -> p c t", c=DC),
                    in_=xt[:, TT * (sl + 1):TT * (sl + 2)].rearrange(
                        "(c p) t -> p c t", p=128)))
                pre = [tyn, tok_k, tok_gn] + ([q_toks[-1]] if own else [])
                ln_fin = ln_pass(x_ts[npar], y2b, xn, pfree=tok_va[-1],
                                 pre_s=pre, out_ready=tok_va[-1], defer=True)

            ktoks = []
            tprev = None

            def upd_round(tc):
                wait("t", ktoks[tc])
                wait("t", tok_va[tc])
                if sl == NO and tc == 0:
                    wait("t", st8["SrecEvac"])
                for p in range(NP):
                    stt = st8["upd_first"] and p in (0, 3, 6)
                    tu = op("t", lambda e, p=p, tc=tc, stt=stt: e.matmul(
                        pS_blk(p), k_tm[:, NP * C * tc + 128 * p:NP * C * tc + 128 * (p + 1)],
                        v_aug[:, SQW * tc + PW * p:SQW * tc + PW * (p + 1)],
                        start=stt, stop=False, skip_group_check=True),
                            1 if p == NP - 1 else 0)
                st8["upd_first"] = False
                return tu

            for tc in range(NCH):
                if tc == 0:
                    wait("t", tok_k)
                if tprev is not None:
                    wait("t", tprev)
                for p in range(NP):
                    tt = op("t", lambda e, p=p, tc=tc: e.transpose(
                        pT[:, 128 * p:128 * (p + 1)],
                        K[:, TT * p + C * tc:TT * p + C * (tc + 1)], ident[:, :]),
                            1 if p == NP - 1 else 0)
                wait("v", tt)
                tprev = op("v", lambda e, tc=tc: e.tensor_copy(
                    k_tm[:, NP * C * tc:NP * C * (tc + 1)], pT[:, :]), 1)
                ktoks.append(tprev)
                if tc == 2 and ln_fin is not None:
                    pend_xn1 = ln_fin()[1]
                if tc >= 1:
                    tok_upd_t = upd_round(tc - 1)
            tok_upd_t = upd_round(NCH - 1)
            tok_ktm = tprev

            if sl == NO - 1:
                wait("v", tok_upd_t)
                for i, pt in enumerate(pS):
                    npp = 3 - (i == 2)
                    op("v", lambda e, i=i, pt=pt, npp=npp: e.tensor_copy(
                        s_rec[:, 390 * i:390 * i + npp * PW], pt[:, 0:npp * PW]))
                st8["SrecEvac"] = op("v", lambda e: e.engine_nop(), 1)
                st8["upd_first"] = True

            xt_readers[par] = [tok_xn2] + ([bt_x2] if bt_x2 else [])

        wait("v", tok_upd_t)
        for i, pt in enumerate(pS):
            npp = 3 - (i == 2)
            op("v", lambda e, i=i, pt=pt, npp=npp: e.tensor_copy(
                s_own[:, 390 * i:390 * i + npp * PW], pt[:, 0:npp * PW]))
        tok_SownEvac = op("v", lambda e: e.engine_nop(), 1)

        wait("g", tok_SownEvac)
        shv = shd.ap().rearrange("(a p) c -> a p c", p=256)
        te1 = dma("g", lambda e: e.dma_start(
            out=shv[bass.ts(st8["g"][0], 1), 0:128, :], in_=s_own[:, :]))
        te2 = dma("g", lambda e: e.dma_start(
            out=shv[bass.ts(st8["g"][0], 1), 128:256, :], in_=s_rec[:, :]))
        wait("g", te1)
        wait("g", te2)
        if cx.live == "g" and not SIM_LOCAL_SYNC:
            e = cx.eng
            par_, opar_, my_flag, pa_flag = st8["g"]
            nonce = e.value_load(nonce_i[0:1, 0:1])
            e.store(my_flag, nonce)

            def cond():
                return e.value_load(pa_flag) != nonce

            with e.While(cond):
                e.nop()
        rdix = 0 if SIM_LOCAL_SYNC else 1
        tp1 = dma("g", lambda e: e.dma_start(
            out=E1p[:, :], in_=shv[bass.ts(st8["g"][rdix], 1), 0:128, :]))
        tp2 = dma("g", lambda e: e.dma_start(
            out=E2p[:, :], in_=shv[bass.ts(st8["g"][rdix], 1), 128:256, :]))
        wait("v", tp1)
        wait("v", tp2)
        op("v", lambda e: e.tensor_scalar_mul(S_start[:, :], s_rec[:, :], mcols[:, 0:1]))
        op("v", lambda e: e.scalar_tensor_tensor(
            S_start[:, :], E2p[:, :], mcols[:, 1:2], S_start[:, :], ALU.mult, ALU.add))
        tok_sstart = op("v", lambda e: e.scalar_tensor_tensor(
            S_start[:, :], E1p[:, :], mcols[:, 2:3], S_start[:, :], ALU.mult, ALU.add), 1)
        if DEBUG:
            wait("g", tok_sstart)
            dma("g", lambda e: e.dma_start(out=dbg["d_sstart"], in_=S_start[:, :]))

        st8["upd_first"] = True
        tok_updprev = tok("t")
        tok_amcp = tok("v")
        tok_ssb = [None, None]
        lastmask = [None, None]
        ste = S_start[0:64, :].rearrange("p (pp a) -> p pp a", a=PW)
        sto = S_start[64:128, :].rearrange("p (pp a) -> p pp a", a=PW)

        def build_ssb(n, first):
            sb_ = S_sbs[n % 2]
            sse = sb_[0:64, :].rearrange("p (pp a) -> p pp a", a=PW)
            sso = sb_[64:128, :].rearrange("p (pp a) -> p pp a", a=PW)
            if first:
                wait("v", tok_sstart)
                op("v", lambda e, sse=sse: e.tensor_copy(sse[:, :, 0:AUG], ste[:, :, 0:AUG]))
                return op("v", lambda e, sso=sso: e.tensor_copy(
                    sso[:, :, AUG:PW], sto[:, :, AUG:PW]), 1)
            wait("v", tok_updprev)
            for i, pt in enumerate(pS):
                npp = 3 - (i == 2)
                pe4 = pt[:, 0:npp * PW].rearrange("p (pp a) -> p pp a", a=PW)
                op("v", lambda e, pe4=pe4, npp=npp, i=i, sse=sse: e.tensor_add(
                    sse[:, 3 * i:3 * i + npp, 0:AUG],
                    pe4[0:64, 0:npp, 0:AUG],
                    ste[:, 3 * i:3 * i + npp, 0:AUG]), 1, chain=(i == 0))
                op("v", lambda e, pe4=pe4, npp=npp, i=i, sso=sso: e.tensor_add(
                    sso[:, 3 * i:3 * i + npp, AUG:PW],
                    pe4[64:128, 0:npp, AUG:PW],
                    sto[:, 3 * i:3 * i + npp, AUG:PW]), 1, chain=False)
            return op("v", lambda e: e.engine_nop(), 1)

        def ktm_round(tok_kload, pt_free, tc2):
            wait("t", tok_kload)
            wait("t", pt_free)
            for p in range(NP):
                tt5 = op("t", lambda e, p=p, tc2=tc2: e.transpose(
                    pT[:, 128 * p:128 * (p + 1)],
                    K[:, TT * p + C * tc2:TT * p + C * (tc2 + 1)], ident[:, :]),
                        1 if p == NP - 1 else 0)
            wait("v", tt5)
            return op("v", lambda e, tc2=tc2: e.tensor_copy(
                k_tm[:, NP * C * tc2:NP * C * (tc2 + 1)], pT[:, :]), 1)

        def ktm_all(tok_kload, pt_free):
            tok_c = pt_free
            for tc2 in range(NCH):
                tok_c = ktm_round(tok_kload, tok_c, tc2)
            return tok_c

        wait("g", tok_upd_t)
        for tk in bounce_qkv:
            wait("g", tk)
        tok_ldq = dma("g", lambda e: e.dma_start(
            out=Q.ap().rearrange("p (c t) -> p c t", c=DC),
            in_=bq.ap()[:, 0:TT].rearrange("(c p) t -> p c t", p=128)))
        tok_ldk = dma("g", lambda e: e.dma_start(
            out=K.ap().rearrange("p (c t) -> p c t", c=DC),
            in_=bk.ap()[:, 0:TT].rearrange("(c p) t -> p c t", p=128)))
        tok_ldv = dma("g", lambda e: e.dma_start(
            out=v_aug[:, :], in_=bva.ap()[:, 0:NCH * SQW]))
        for tk in xt_readers[0]:
            wait("g", tk)
        xt_readers[0] = []
        tok_ldx = dma("g", lambda e: e.dma_start(
            out=x_tA.ap().rearrange("p (c t) -> p c t", c=DC),
            in_=bx2.ap()[:, 0:TT].rearrange("(c p) t -> p c t", p=128)))
        tok_ssb[0] = build_ssb(0, True)
        tok_ktm = ktm_all(tok_ldk, tok("v"))
        tok_out = tok("v")

        for j in range(NO):
            par = j % 2
            xb = x_ts[par]
            for tc in range(NCH):
                n = j * NCH + tc
                sp = n % 2
                if tc == 0:
                    wait("t", tok_ldv)
                    wait("t", tok_ktm)
                sb_cur = S_sbs[sp]

                def emit_outs(hh, tok_m):
                    p2 = hh // 2
                    wait("t", tok_m)
                    ob = pC if hh < 8 else pD
                    oreg = 64 * (hh % 8)
                    op("t", lambda e, hh=hh, tc=tc, ob=ob, oreg=oreg: e.matmul(
                        ob[:, oreg:oreg + DH], s_Am[:, C * (hh % 2):C * (hh % 2 + 1)],
                        v_aug[:, SQW * tc + AUG * hh:SQW * tc + AUG * hh + DH],
                        start=(hh in (0, 8)), stop=False, skip_group_check=True))
                    op("t", lambda e, hh=hh, tc=tc: e.matmul(
                        pS2[:, 400 + hh:401 + hh],
                        s_Am[:, C * (hh % 2):C * (hh % 2 + 1)],
                        v_aug[:, SQW * tc + AUG * hh + DH:SQW * tc + AUG * (hh + 1)],
                        start=False, stop=False, skip_group_check=True))
                    op("t", lambda e, hh=hh, p2=p2, tc=tc, ob=ob, oreg=oreg, sb_=sb_cur: e.matmul(
                        ob[:, oreg:oreg + DH], Q[:, TT * p2 + C * tc:TT * p2 + C * (tc + 1)],
                        sb_[:, AUG * hh:AUG * hh + DH],
                        start=False, stop=False, skip_group_check=True))
                    return op("t", lambda e, hh=hh, p2=p2, tc=tc, sb_=sb_cur: e.matmul(
                        pS2[:, 400 + hh:401 + hh],
                        Q[:, TT * p2 + C * tc:TT * p2 + C * (tc + 1)],
                        sb_[:, AUG * hh + DH:AUG * (hh + 1)],
                        start=False, stop=False, skip_group_check=True),
                        1 if hh == H - 1 else 0)

                masks = [None] * H
                for hh in range(H):
                    p2 = hh // 2
                    bb = 64 * (hh % 2)
                    bank = pA if hh % 2 == 0 else pB
                    if hh == 0:
                        wait("t", tok_ssb[sp])
                        wait("t", tok_den_free if n > 0 else tok_sstart)
                        wait("t", tok_ldq)
                        wait("t", tok_out)
                    if lastmask[hh % 2] is not None:
                        wait("t", lastmask[hh % 2])
                    tt = op("t", lambda e, p2=p2, bb=bb, tc=tc, bank=bank: e.matmul(
                        bank[:, 0:C], K[bb:bb + 64, TT * p2 + C * tc:TT * p2 + C * (tc + 1)],
                        Q[bb:bb + 64, TT * p2 + C * tc:TT * p2 + C * (tc + 1)],
                        start=True, stop=True, skip_group_check=True), 1)
                    wait("v", tt)
                    masks[hh] = op("v", lambda e, hh=hh, bank=bank: e.tensor_mul(
                        s_Am[:, C * (hh % 2):C * (hh % 2 + 1)], bank[:, 0:C],
                        mask_tri[:, :]), 1)
                    lastmask[hh % 2] = masks[hh]
                    if hh >= 1:
                        tok_heads = emit_outs(hh - 1, masks[hh - 1])
                tok_heads = emit_outs(H - 1, masks[H - 1])
                for p in range(6):
                    stt = st8["upd_first"] and p in (0, 3)
                    op("t", lambda e, p=p, tc=tc, stt=stt: e.matmul(
                        pS_blk(p), k_tm[:, NP * C * tc + 128 * p:NP * C * tc + 128 * (p + 1)],
                        v_aug[:, SQW * tc + PW * p:SQW * tc + PW * (p + 1)],
                        start=stt, stop=False, skip_group_check=True))
                wait("v", tok_heads)
                op("v", lambda e: e.tensor_scalar_add(den_sb[:, :], pS2[:, 400:416], EPS))
                tvd = op("v", lambda e: e.reciprocal(rd_sb[:, :], den_sb[:, :]), 1)
                tok_den_free = op("v", lambda e: e.memset(pS2[:, 400:416], 0.0), 1)
                for hh in range(H):
                    ob = pC if hh < 8 else pD
                    oreg = 64 * (hh % 8)
                    op("v", lambda e, hh=hh, ob=ob, oreg=oreg: e.tensor_scalar_mul(
                        tmp[:, 64 * hh:64 * (hh + 1)], ob[:, oreg:oreg + DH],
                        rd_sb[:, hh:hh + 1]), 1, chain=(hh == 0))
                tok_otm = op("v", lambda e: e.engine_nop(), 1)
                if DEBUG and j == 0 and tc == 0:
                    wait("g", tok_otm)
                    dma("g", lambda e: e.dma_start(out=dbg["d_otm"], in_=tmp[:, 0:1024]))
                wait("t", tok_den_free)
                for p in range(6, NP):
                    stt = st8["upd_first"] and p == 6
                    tok_updprev = op("t", lambda e, p=p, tc=tc, stt=stt: e.matmul(
                        pS_blk(p), k_tm[:, NP * C * tc + 128 * p:NP * C * tc + 128 * (p + 1)],
                        v_aug[:, SQW * tc + PW * p:SQW * tc + PW * (p + 1)],
                        start=stt, stop=False, skip_group_check=True),
                            1 if p == NP - 1 else 0)
                st8["upd_first"] = False
                if n + 1 < NO * NCH:
                    tok_ssb[(n + 1) % 2] = build_ssb(n + 1, False)
                wait("t", tok_otm)
                for p in range(NP):
                    tt2 = op("t", lambda e, p=p: e.transpose(
                        pT[:, 128 * p:128 * (p + 1)], tmp[:, 128 * p:128 * (p + 1)],
                        ident[:, :]), 1 if p == NP - 1 else 0)
                wait("v", tt2)
                tok_amcp = op("v", lambda e, tc=tc: e.tensor_copy(
                    gn.ap().rearrange("p (pp t) -> p pp t", pp=NP)[:, :, C * tc:C * (tc + 1)],
                    pT.ap().rearrange("p (pp t) -> p pp t", pp=NP)), 1)

            if j + 1 < NO:
                wait("g", tok_updprev)
                tok_ldk = dma("g", lambda e, j=j: e.dma_start(
                    out=K.ap().rearrange("p (c t) -> p c t", c=DC),
                    in_=bk.ap()[:, TT * (j + 1):TT * (j + 2)].rearrange(
                        "(c p) t -> p c t", p=128)))
                tok_ldq = dma("g", lambda e, j=j: e.dma_start(
                    out=Q.ap().rearrange("p (c t) -> p c t", c=DC),
                    in_=bq.ap()[:, TT * (j + 1):TT * (j + 2)].rearrange(
                        "(c p) t -> p c t", p=128)))
                tok_ldv = dma("g", lambda e, j=j: e.dma_start(
                    out=v_aug[:, :],
                    in_=bva.ap()[:, NCH * SQW * (j + 1):NCH * SQW * (j + 2)]))
                npar = (j + 1) % 2
                for tk in xt_readers[npar]:
                    wait("g", tk)
                xt_readers[npar] = []
                tok_ldx = dma("g", lambda e, j=j, npar=npar: e.dma_start(
                    out=x_ts[npar].ap().rearrange("p (c t) -> p c t", c=DC),
                    in_=bx2.ap()[:, TT * (j + 1):TT * (j + 2)].rearrange(
                        "(c p) t -> p c t", p=128)))

            tok_pj = [None, None]
            pld = {}
            for po in range(2):
                wait("y", st8["wo_free"][po % 2])
                pld[po] = dma("y", lambda e, po=po: e.dma_start(
                    out=wo_sb[:, 1024 * (po % 2):1024 * (po % 2 + 1)],
                    in_=proj_sw.ap()[:, 1024 * po:1024 * (po + 1)]))
            for po in range(DC):
                ppar = po % 2
                if po >= 2:
                    wait("y", st8["wo_free"][ppar])
                    pld[po] = dma("y", lambda e, po=po, ppar=ppar: e.dma_start(
                        out=wo_sb[:, 1024 * ppar:1024 * (ppar + 1)],
                        in_=proj_sw.ap()[:, 1024 * po:1024 * (po + 1)]))
                ty = pld[po]
                wait("t", ty)
                if po == 0:
                    wait("t", tok_amcp)
                    wait("t", tok_ldx)
                bank = pA if ppar == 0 else pB
                if tok_pj[ppar] is not None:
                    wait("t", tok_pj[ppar])
                for c in range(DC):
                    tt = op("t", lambda e, c=c, ppar=ppar, bank=bank: e.matmul(
                        bank[:, :], wo_sb[:, 1024 * ppar + 128 * c:1024 * ppar + 128 * (c + 1)],
                        gn[:, TT * c:TT * (c + 1)],
                        start=(c == 0), stop=(c == DC - 1), skip_group_check=True),
                            1 if c == DC - 1 else 0)
                st8["wo_free"][ppar] = tt
                wait("v", tt)
                tok_pj[ppar] = op("v", lambda e, po=po, bank=bank, xb=xb: e.scalar_tensor_tensor(
                    xb[:, TT * po:TT * (po + 1)], bank[:, :], proj_bc[:, po:po + 1],
                    xb[:, TT * po:TT * (po + 1)], ALU.add, ALU.add), 1)
                if 4 <= po < 4 + NCH and j + 1 < NO:
                    tok_ktm = ktm_round(tok_ldk, tok_amcp if po == 4 else tok_ktm,
                                        po - 4)
            tok_x3 = op("v", lambda e: e.engine_nop(), 1)
            tok_xn3s = ln_pass(xb, y2b, xn, pfree=tok_x3,
                               pre_s=[tok_x3, tok_amcp])[1]
            tok_out = ffn_pass(xb, tok_xn3s)
            wait("g", tok_out)
            _to = dma("g", lambda e, j=j, xb=xb: e.dma_start(
                out=out_t[:, TT * j:TT * (j + 1)].rearrange("(c p) t -> p c t", p=128),
                in_=xb.ap().rearrange("p (c t) -> p c t", c=DC)))
            xt_readers[par] = [_to, tok_out]


    sems = {}
    with ctxm:
        with ExitStack() as semstack:
            names = ["t", "v", "s"] + ["y%d" % i for i in range(NLANE)] + \
                ["g%d" % i for i in range(NLANE)]
            for n in names:
                sems[n] = semstack.enter_context(nc.semaphore("sem_" + n))
            with nc.Block() as block:
                @block.tensor
                def _(eng):
                    schedule(Ctx("t", eng, sems))

                @block.vector
                def _(eng):
                    schedule(Ctx("v", eng, sems))

                @block.scalar
                def _(eng):
                    schedule(Ctx("s", eng, sems))

                @block.sync
                def _(eng):
                    schedule(Ctx("y", eng, sems))

                @block.gpsimd
                def _(eng):
                    schedule(Ctx("g", eng, sems))

    return nc


def stage_inputs(inputs):
    x = np.ascontiguousarray(np.asarray(inputs["L0_attn_x"], np.float32))
    qkv_b = np.asarray(inputs["L0_attn_qkv_b"], np.float32)
    wg_i = np.ascontiguousarray(np.asarray(inputs["L0_ffn_w_gate"], np.int32).T)
    wu_i = np.ascontiguousarray(np.asarray(inputs["L0_ffn_w_up"], np.int32).T)
    wd_i = np.ascontiguousarray(np.asarray(inputs["L0_ffn_w_down"], np.int32).T)
    qkvT = np.ascontiguousarray(np.asarray(inputs["L0_attn_qkv_w"], np.float32).T)
    gateT = np.ascontiguousarray(np.asarray(inputs["L0_attn_gate_w"], np.float32).T)
    projT = np.ascontiguousarray(np.asarray(inputs["L0_attn_proj_w"], np.float32).T)

    def col(v, n):
        return np.ascontiguousarray(np.broadcast_to(v.reshape(n, 128).T, (128, n)))

    qkv_bc = col(qkv_b[0:2048], 16)
    vb_row = np.ascontiguousarray(qkv_b[2048:3072].reshape(1, D))
    gate_bc = col(np.asarray(inputs["L0_attn_gate_b"], np.float32), DC)
    proj_bc = col(np.asarray(inputs["L0_attn_proj_b"], np.float32), DC)
    ln_gc = col(np.asarray(inputs["L0_attn_ln_g"], np.float32), DC)
    ln_bc = col(np.asarray(inputs["L0_attn_ln_b"], np.float32), DC)
    mask_tri = np.triu(np.ones((C, C), np.float32))
    identm = np.eye(C).astype(ml_dtypes.bfloat16)
    masks = {0: (0., 0., 0.), 1: (0., 0., 1.), 2: (1., 1., 0.), 3: (1., 1., 1.)}
    consts = np.ascontiguousarray(
        np.broadcast_to(np.array([EPS, -0.5], np.float32), (128, 2)))
    import os as _os
    nonce = (int.from_bytes(_os.urandom(4), "little") & 0x7FFFFFFF) | 1
    nonce_arr = np.full((1, 16), nonce, np.int32)

    in_maps = []
    OWN = T // 4
    for c in range(8):
        b, r = c // 4, c % 4
        own = x[b, r * OWN:(r + 1) * OWN]
        if r == 2:
            rec = x[b, 0:OWN]
        elif r == 3:
            rec = x[b, OWN:2 * OWN]
        else:
            rec = np.zeros_like(own)
        xt = np.ascontiguousarray(np.concatenate([rec, own], axis=0).T)
        mc = np.ascontiguousarray(
            np.broadcast_to(np.array(masks[r], np.float32), (128, 3)))
        in_maps.append({
            "xt": xt, "wgT_i": wg_i, "wuT_i": wu_i, "wdT_i": wd_i,
            "qkvT": qkvT, "gateT": gateT, "projT": projT,
            "qkv_bc": qkv_bc, "vb_row": vb_row, "gate_bc": gate_bc,
            "proj_bc": proj_bc, "ln_gc": ln_gc, "ln_bc": ln_bc,
            "mcols": mc, "mask_tri": mask_tri, "ident": identm,
            "consts_i": consts, "nonce_i": nonce_arr,
        })
    return in_maps


def kernel(**inputs):
    in_maps = stage_inputs(inputs)
    nc = build_program()
    res = run_bass_kernel_spmd(nc, in_maps, core_ids=list(range(8)))
    out = np.zeros((B, T, D), np.float32)
    OWN = T // 4
    for c in range(8):
        b, r = c // 4, c % 4
        out[b, r * OWN:(r + 1) * OWN] = res.results[c]["out_t"].T
    return out
